# version 02 (final)
# baseline (speedup 1.0000x reference)
import numpy as np
import concourse.bass as bass
import concourse.mybir as mybir
from concourse.bass_utils import run_bass_kernel_spmd

F32 = mybir.dt.float32
BF16 = mybir.dt.bfloat16
AF = mybir.ActivationFunctionType
ALU = mybir.AluOpType
AX = mybir.AxisListType

NCORES = 8
T = 4096
DM = 1024
DC = DM // 128
TT = 512
NTT = T // TT
LN_EPS = 1e-5

ENGINES = ("tensor", "vector", "scalar", "gpsimd", "sync")


def ceil_div(a, b):
    return -(-a // b)


class Region:
    __slots__ = ("name", "writer", "readers")

    def __init__(self, name):
        self.name = name
        self.writer = None
        self.readers = []


class Buf:
    __slots__ = ("t", "r", "shape", "dtype")

    def __init__(self, t, name, shape, dtype):
        self.t = t
        self.r = Region(name)
        self.shape = shape
        self.dtype = dtype


class Step:
    __slots__ = ("engine", "fn", "deps", "kind", "value_after", "waits", "tag",
                 "slot", "self_wait")

    def __init__(self, engine, fn, deps, kind, value_after, tag):
        self.engine = engine
        self.fn = fn
        self.deps = deps
        self.kind = kind
        self.value_after = value_after
        self.waits = []
        self.tag = tag


PARANOID = [False]


class Prog:
    def __init__(self, nc):
        self.nc = nc
        self.steps = {e: [] for e in ENGINES}
        self.ec = {e: 0 for e in ENGINES}
        self.dc = {e: 0 for e in ENGINES}
        self._ctxs = []
        self.sem = {e: self._sem("S" + e[:2]) for e in ENGINES}
        self.NSLOT = 8
        self.dmasem = {e: [self._sem(f"D{e[:2]}{i}") for i in range(self.NSLOT)]
                       for e in ("sync", "gpsimd", "scalar", "vector")}
        self.dslot = {e: [0] * self.NSLOT for e in self.dmasem}
        self.didx = {e: 0 for e in self.dmasem}

    def _sem(self, name):
        ctx = self.nc.semaphore(name)
        h = ctx.__enter__()
        self._ctxs.append(ctx)
        return h

    def sb(self, name, shape, dtype):
        t = self.nc.alloc_sbuf_tensor(name, list(shape), dtype)
        return Buf(t, name, tuple(shape), dtype)

    def psum(self, name, shape, dtype):
        ctx = self.nc.psum_tensor(name, list(shape), dtype)
        t = ctx.__enter__()
        self._ctxs.append(ctx)
        return Buf(t, name, tuple(shape), dtype)

    def dram(self, name, shape, dtype, kind="Internal"):
        t = self.nc.dram_tensor(name, list(shape), dtype, kind=kind)
        return Buf(t.ap(), name, tuple(shape), dtype)

    def arena_reserve(self, name, nbytes):
        t = self.nc.alloc_sbuf_tensor(name, [128, nbytes], mybir.dt.uint8)
        mloc = self.nc.lookup_mloc(t)
        return mloc.addr

    def sb_at(self, name, shape, dtype, offset):
        t = self.nc.alloc_sbuf_tensor_at(name, list(shape), dtype, offset=offset)
        return Buf(t, name, tuple(shape), dtype)

    def handoff(self, old_bufs, new_bufs):
        toks = set()
        for b in old_bufs:
            if b.r.writer is not None:
                toks.add(b.r.writer)
            toks.update(b.r.readers)
        for b in new_bufs:
            b.r.readers = list(set(b.r.readers) | toks)

    def step(self, engine, fn, reads=(), writes=(), tag=""):
        deps = self._deps(reads, writes, tag)
        self.ec[engine] += 1
        st = Step(engine, fn, deps, "c", self.ec[engine], tag)
        self._update(( engine, "c", self.ec[engine]), reads, writes)
        self.steps[engine].append(st)
        return (engine, "c", self.ec[engine])

    def dma(self, engine, fn, n, reads=(), writes=(), tag=""):
        deps = self._deps(reads, writes, tag)
        slot = self.didx[engine] % self.NSLOT
        self.didx[engine] += 1
        prev_total = self.dslot[engine][slot]
        self.dslot[engine][slot] += 16 * n
        self.dc[engine] += 16 * n
        st = Step(engine, fn, deps, "d", self.dslot[engine][slot], tag)
        st.slot = slot
        st.self_wait = prev_total
        tok = (engine, "d", slot, self.dslot[engine][slot])
        self._update(tok, reads, writes)
        self.steps[engine].append(st)
        return tok

    def _tails(self):
        toks = set()
        for e in ENGINES:
            if self.ec[e]:
                toks.add((e, "c", self.ec[e]))
        for e in self.dmasem:
            for s in range(self.NSLOT):
                if self.dslot[e][s]:
                    toks.add((e, "d", s, self.dslot[e][s]))
        return toks

    def _deps(self, reads, writes, tag=""):
        pz = PARANOID[0]
        if pz is True or (callable(pz) and pz(tag)):
            return self._tails()
        deps = set()
        for r in reads:
            rr = r.r if isinstance(r, Buf) else r
            if rr.writer is not None:
                deps.add(rr.writer)
        for w in writes:
            ww = w.r if isinstance(w, Buf) else w
            if ww.writer is not None:
                deps.add(ww.writer)
            deps.update(ww.readers)
        return deps

    def _update(self, tok, reads, writes):
        for r in reads:
            rr = r.r if isinstance(r, Buf) else r
            rr.readers.append(tok)
        for w in writes:
            ww = w.r if isinstance(w, Buf) else w
            ww.writer = tok
            ww.readers = []

    def emit(self):
        tails = []
        for e in ENGINES:
            if self.ec[e] and not (e == "sync"):
                tails.append((e, "c", self.ec[e]))
        for e in self.dmasem:
            for s in range(self.NSLOT):
                if self.dslot[e][s]:
                    tails.append((e, "d", s, self.dslot[e][s]))
        self.ec["sync"] += 1
        fin = Step("sync", lambda eng: eng.nop(), set(tails), "c",
                   self.ec["sync"], "finish")
        self.steps["sync"].append(fin)

        for e in ENGINES:
            hi = {}
            for st in self.steps[e]:
                want = {}
                for tok in st.deps:
                    if tok[1] == "c":
                        k = (tok[0], "c")
                        val = tok[2]
                    else:
                        k = (tok[0], "d", tok[2])
                        val = tok[3]
                    if val <= hi.get(k, 0):
                        continue
                    want[k] = max(want.get(k, 0), val)
                for k, val in sorted(want.items()):
                    if k[1] == "c":
                        sem = self.sem[k[0]]
                    else:
                        sem = self.dmasem[k[0]][k[2]]
                    st.waits.append((sem, val))
                    hi[k] = val

        prog = self
        with self.nc.Block() as blk:
            for e in ENGINES:
                if not self.steps[e]:
                    continue

                def make_body(eng_name):
                    def body(eng):
                        for st in prog.steps[eng_name]:
                            for sem, val in st.waits:
                                eng.wait_ge(sem, val)
                            if st.kind == "c":
                                last = st.fn(eng)
                                if last is None:
                                    eng.drain()
                                    last = eng.nop()
                                last.then_inc(prog.sem[eng_name], 1)
                            else:
                                if st.self_wait > 0:
                                    eng.wait_ge(prog.dmasem[eng_name][st.slot],
                                                st.self_wait)
                                st.fn(eng, prog.dmasem[eng_name][st.slot])
                    return body

                getattr(blk, e)(make_body(e))



def g_cast(p, dst_ap, src_ap, reads, writes, tag="cast", n=1):
    def fn(eng, dsem):
        eng.dma_start(out=dst_ap, in_=src_ap).then_inc(dsem, 16)
    return p.dma("gpsimd", fn, n, reads=reads, writes=writes, tag=tag)


def s_dma(p, dst_ap, src_ap, reads, writes, tag="dma", allow_slow=False):
    def fn(eng, dsem):
        eng.dma_start(out=dst_ap, in_=src_ap,
                      allow_slow_non_contiguous=allow_slow).then_inc(dsem, 16)
    return p.dma("sync", fn, 1, reads=reads, writes=writes, tag=tag)


def prep_weight_T(p, wT, w_dram, ident, pbt, o_dim, d_dim, stage, tag):
    OC = o_dim // 128
    KC = d_dim // 128
    for oc in range(OC):
        g_cast(p, stage.t[:, 0:d_dim], w_dram.t[oc * 128:(oc + 1) * 128, :],
               [w_dram], [stage], tag=f"{tag}ld{oc}")
        for kc in range(KC):
            pb = pbt[(oc * KC + kc) % len(pbt)]

            def ftr(eng, kc=kc, pb=pb):
                return eng.transpose(pb.t[0:128, 0:128],
                                     stage.t[:, kc * 128:(kc + 1) * 128],
                                     ident.t[:, 0:128])
            p.step("tensor", ftr, reads=[stage, ident], writes=[pb],
                   tag=f"{tag}tr{oc}_{kc}")

            def fev(eng, oc=oc, kc=kc, pb=pb):
                return eng.tensor_copy(wT.t[:, kc, oc * 128:(oc + 1) * 128],
                                       pb.t[0:128, 0:128])
            p.step("vector", fev, reads=[pb], writes=[wT], tag=f"{tag}ev{oc}_{kc}")


def load_weight_inmajor(p, wT, w_dram, d_dim, o_dim, tag):
    g_cast(p, wT.t[:, :, :],
           w_dram.t.rearrange("(kc p) o -> p kc o", p=128),
           [w_dram], [wT], tag=tag)


def load_bias_cols(p, dst, b_dram, n, tag):
    s_dma(p, dst.t[:, :], b_dram.t.rearrange("(kc p) -> p kc", p=128),
          [b_dram], [dst], tag=tag, allow_slow=True)


class LNState:

    def __init__(self, p, name, keep_stats=False, inplace_xc=False):
        n = T if keep_stats else TT
        self.keep = keep_stats
        self.mu = p.sb(name + "_mu", [1, n], F32)
        self.rstd = p.sb(name + "_rstd", [1, n], F32)
        self.mub = p.sb(name + "_mub", [128, TT], F32)
        self.rstdb = p.sb(name + "_rstdb", [128, TT], F32)
        self.xb = p.sb(name + "_xb", [128, DC, TT], BF16)
        self.sq = Buf(self.xb.t, name + "_sq", (128, DC, TT), BF16)
        self.sq.r = self.xb.r
        self.inplace_xc = inplace_xc
        if not inplace_xc:
            self.xc = p.sb(name + "_xc", [128, DC, TT], F32)
        self.eps = p.sb(name + "_eps", [1, 1], F32)
        def fe(eng):
            return eng.memset(self.eps.t[:, :], LN_EPS)
        p.step("vector", fe, writes=[self.eps], tag=name + "eps")


def emit_ln_tile(p, ln, xs_tile_ap, xs_buf, z_out_ap, z_buf, tt, pb_stat,
                 pb_bc, ones_col, ones_row, g_cols=None, b_cols=None,
                 tag="ln", xc_buf=None):
    t0 = (tt * TT) if ln.keep else 0
    def fcast(eng):
        last = None
        for kc in range(DC):
            last = eng.copy(ln.xb.t[:, kc, :], xs_tile_ap(kc))
        return last
    p.step("scalar", fcast, reads=[xs_buf], writes=[ln.xb], tag=tag + "cast")

    def fsum(eng):
        last = None
        for kc in range(DC):
            eng.ldweights(ones_col.t[:, 0:1])
            last = eng.matmul(pb_stat.t[0:1, 0:TT], ones_col.t[:, 0:1],
                              ln.xb.t[:, kc, :], start=(kc == 0),
                              stop=(kc == DC - 1))
        return last
    p.step("tensor", fsum, reads=[ln.xb, ones_col], writes=[pb_stat], tag=tag + "sum")

    def fmu(eng):
        return eng.mul(ln.mu.t[0:1, t0:t0 + TT], pb_stat.t[0:1, 0:TT], 1.0 / DM)
    p.step("scalar", fmu, reads=[pb_stat], writes=[ln.mu], tag=tag + "mu")

    def fbc(eng):
        return eng.matmul(pb_bc.t[0:128, 0:TT], ones_row.t[0:1, :],
                          ln.mu.t[0:1, t0:t0 + TT], start=True, stop=True)
    p.step("tensor", fbc, reads=[ln.mu, ones_row], writes=[pb_bc], tag=tag + "bcmu")

    def fmub(eng):
        return eng.tensor_copy(ln.mub.t[:, :], pb_bc.t[0:128, 0:TT])
    p.step("vector", fmub, reads=[pb_bc], writes=[ln.mub], tag=tag + "mubev")

    if xc_buf is not None:
        xcap = lambda kc: xc_buf.t[:, kc, :]
        xcbuf = xc_buf
    elif ln.inplace_xc:
        xcap = xs_tile_ap
        xcbuf = xs_buf
    else:
        xcap = lambda kc: ln.xc.t[:, kc, :]
        xcbuf = ln.xc

    def fxc(eng):
        last = None
        for kc in range(DC):
            last = eng.tensor_sub(xcap(kc), xs_tile_ap(kc), ln.mub.t[:, :])
        return last
    p.step("vector", fxc, reads=[xs_buf, ln.mub], writes=[xcbuf], tag=tag + "xc")

    def fsq(eng):
        last = None
        for kc in range(DC):
            last = eng.activation(out=ln.sq.t[:, kc, :], in_=xcap(kc),
                                  func=AF.Square)
        return last
    p.step("scalar", fsq, reads=[xcbuf], writes=[ln.sq], tag=tag + "sq")

    def fss(eng):
        last = None
        for kc in range(DC):
            eng.ldweights(ones_col.t[:, 0:1])
            last = eng.matmul(pb_stat.t[0:1, 0:TT], ones_col.t[:, 0:1],
                              ln.sq.t[:, kc, :], start=(kc == 0),
                              stop=(kc == DC - 1))
        return last
    p.step("tensor", fss, reads=[ln.sq, ones_col], writes=[pb_stat], tag=tag + "ss")

    def fstd(eng):
        return eng.activation(out=ln.rstd.t[0:1, t0:t0 + TT],
                              in_=pb_stat.t[0:1, 0:TT], func=AF.Sqrt,
                              scale=1.0 / DM, bias=ln.eps.t[0:1, 0:1])
    p.step("scalar", fstd, reads=[pb_stat, ln.eps], writes=[ln.rstd], tag=tag + "std")

    def frcp(eng):
        return eng.reciprocal(ln.rstd.t[0:1, t0:t0 + TT],
                              ln.rstd.t[0:1, t0:t0 + TT])
    p.step("vector", frcp, reads=[ln.rstd], writes=[ln.rstd], tag=tag + "rcp")

    def fbc2(eng):
        return eng.matmul(pb_bc.t[0:128, 0:TT], ones_row.t[0:1, :],
                          ln.rstd.t[0:1, t0:t0 + TT], start=True, stop=True)
    p.step("tensor", fbc2, reads=[ln.rstd, ones_row], writes=[pb_bc], tag=tag + "bcsd")

    def frstdb(eng):
        return eng.tensor_copy(ln.rstdb.t[:, :], pb_bc.t[0:128, 0:TT])
    p.step("vector", frstdb, reads=[pb_bc], writes=[ln.rstdb], tag=tag + "sdev")

    if g_cols is None:
        def fz(eng):
            last = None
            for kc in range(DC):
                last = eng.tensor_mul(z_out_ap(kc), xcap(kc), ln.rstdb.t[:, :])
            return last
        p.step("vector", fz, reads=[xcbuf, ln.rstdb], writes=[z_buf], tag=tag + "z")
    else:
        def fz1(eng):
            last = None
            for kc in range(DC):
                last = eng.tensor_mul(xcap(kc), xcap(kc), ln.rstdb.t[:, :])
            return last
        p.step("vector", fz1, reads=[xcbuf, ln.rstdb], writes=[xcbuf], tag=tag + "z1")

        def fz2(eng):
            last = None
            for kc in range(DC):
                last = eng.tensor_scalar(
                    z_out_ap(kc), xcap(kc),
                    g_cols.t[:, kc:kc + 1], b_cols.t[:, kc:kc + 1],
                    ALU.mult, ALU.add)
            return last
        p.step("vector", fz2, reads=[xcbuf, g_cols, b_cols], writes=[z_buf],
               tag=tag + "z2")



def build_ffn0():
    nc = bass.Bass("TRN2", target_bir_lowering=False, debug=False, detect_race_conditions=False)
    p = Prog(nc)

    x_in = p.dram("x_in", [DM, T], F32, kind="ExternalInput")
    W = p.dram("Wf", [DM, DM], F32, kind="ExternalInput")
    bvec = p.dram("bf", [DM], F32, kind="ExternalInput")
    V_r = p.dram("V_rf", [DM, 32], F32, kind="ExternalInput")
    C = p.dram("Cf", [DM, 32], F32, kind="ExternalInput")
    ident_d = p.dram("c_ident", [128, 128], F32, kind="ExternalInput")
    x_out = p.dram("x_out", [DM, T], F32, kind="ExternalOutput")

    xs = p.sb("xs", [128, DC, T], F32)
    z = p.sb("z", [128, DC, TT], BF16)
    wT = p.sb("wT", [128, DC, DM], BF16)
    stage = p.sb("stage", [128, DM], BF16)
    vrT = p.sb("vrT", [128, DC, 32], BF16)
    cT = p.sb("cT", [32, DM], BF16)
    cstage = p.sb("cstage", [128, 8, 32], BF16)
    identb = p.sb("identb", [128, 128], BF16)
    ones_col = p.sb("ones_col", [128, 1], BF16)
    ones_row = p.sb("ones_row", [1, 128], F32)
    bias05 = p.sb("bias05", [128, DC], F32)
    rfm = p.sb("rfm", [32, TT], BF16)
    otile = p.sb("otile", [128, TT], F32)
    ln = LNState(p, "ln")

    pb = [p.psum(f"pb{i}", [128, 512], F32) for i in range(6)]
    pbt = [p.psum(f"pbt{i}", [128, 128], BF16) for i in range(2)]

    g_cast(p, identb.t[:, :], ident_d.t[:], [ident_d], [identb], tag="identb")

    def f1(eng):
        return eng.memset(ones_col.t[:, :], 1.0)
    p.step("vector", f1, writes=[ones_col], tag="ones_col")

    def f2(eng):
        return eng.memset(ones_row.t[:, :], 1.0)
    p.step("vector", f2, writes=[ones_row], tag="ones_row")

    prep_weight_T(p, wT, W, identb, pbt, DM, DM, stage, "W")
    load_weight_inmajor(p, vrT, V_r, DM, 32, "vr")
    g_cast(p, cstage.t[:, :, :], C.t.rearrange("(oc p) r -> p oc r", p=128),
           [C], [cstage], tag="cld")
    for oc in range(DC):
        pbx = pbt[oc % 2]

        def ftr(eng, oc=oc, pbx=pbx):
            return eng.transpose(pbx.t[0:32, 0:128], cstage.t[:, oc, :],
                                 identb.t[:, 0:128])
        p.step("tensor", ftr, reads=[cstage, identb], writes=[pbx], tag=f"ctr{oc}")

        def fev(eng, oc=oc, pbx=pbx):
            return eng.tensor_copy(cT.t[0:32, oc * 128:(oc + 1) * 128],
                                   pbx.t[0:32, 0:128])
        p.step("vector", fev, reads=[pbx], writes=[cT], tag=f"cev{oc}")

    load_bias_cols(p, bias05, bvec, DM, "bias")

    def fb5(eng):
        return eng.mul(bias05.t[:, :], bias05.t[:, :], 0.5)
    p.step("scalar", fb5, reads=[bias05], writes=[bias05], tag="b05")

    s_dma(p, xs.t[:, :, :], x_in.t.rearrange("(kc p) t -> p kc t", p=128),
          [x_in], [xs], tag="xld")

    for tt in range(NTT):
        t0 = tt * TT

        def xs_ap(kc, t0=t0):
            return xs.t[:, kc, t0:t0 + TT]

        def z_ap(kc):
            return z.t[:, kc, :]

        emit_ln_tile(p, ln, xs_ap, xs, z_ap, z, tt, pb[0], pb[1], ones_col,
                     ones_row, tag=f"ln{tt}_")

        def frk(eng):
            last = None
            for kc in range(DC):
                eng.ldweights(vrT.t[:, kc, :])
                last = eng.matmul(pb[2].t[0:32, 0:TT], vrT.t[:, kc, :],
                                  z.t[:, kc, :], start=(kc == 0),
                                  stop=(kc == DC - 1))
            return last
        p.step("tensor", frk, reads=[vrT, z], writes=[pb[2]], tag=f"rk{tt}")

        def frev(eng):
            return eng.tensor_copy(rfm.t[0:32, :], pb[2].t[0:32, 0:TT])
        p.step("vector", frev, reads=[pb[2]], writes=[rfm], tag=f"rkev{tt}")

        for oc in range(DC):
            pbo = pb[3 + (oc % 3)]

            def fmm(eng, oc=oc, pbo=pbo):
                last = None
                for kc in range(DC):
                    eng.ldweights(wT.t[:, kc, oc * 128:(oc + 1) * 128])
                    last = eng.matmul(pbo.t[0:128, 0:TT],
                                      wT.t[:, kc, oc * 128:(oc + 1) * 128],
                                      z.t[:, kc, :], start=(kc == 0), stop=False)
                eng.ldweights(cT.t[0:32, oc * 128:(oc + 1) * 128])
                last = eng.matmul(pbo.t[0:128, 0:TT],
                                  cT.t[0:32, oc * 128:(oc + 1) * 128],
                                  rfm.t[0:32, :], start=False, stop=True)
                return last
            p.step("tensor", fmm, reads=[wT, z, cT, rfm], writes=[pbo],
                   tag=f"mm{tt}_{oc}")

            def fep(eng, oc=oc, pbo=pbo):
                return eng.activation(out=otile.t[:, 0:TT], in_=pbo.t[0:128, 0:TT],
                                      func=AF.Identity, bias=bias05.t[:, oc:oc + 1],
                                      scale=0.5)
            p.step("scalar", fep, reads=[pbo, bias05], writes=[otile],
                   tag=f"ep{tt}_{oc}")

            def fadd(eng, oc=oc, t0=t0):
                return eng.tensor_add(xs.t[:, oc, t0:t0 + TT],
                                      xs.t[:, oc, t0:t0 + TT], otile.t[:, 0:TT])
            p.step("vector", fadd, reads=[otile, xs], writes=[xs],
                   tag=f"xadd{tt}_{oc}")

    s_dma(p, x_out.t.rearrange("(kc p) t -> p kc t", p=128), xs.t[:, :, :],
          [xs], [x_out], tag="xst")

    p.emit()
    return nc




INNER = 2048
ICC = INNER // 128
DS = 64


def build_l1attn():
    nc = bass.Bass("TRN2", target_bir_lowering=False, debug=False, detect_race_conditions=False)
    p = Prog(nc)
    old_par = PARANOID[0]

    x_in = p.dram("x_in", [DM, T], F32, kind="ExternalInput")
    W_in = p.dram("W_in", [DM, 2 * INNER], F32, kind="ExternalInput")
    b_in = p.dram("b_in", [2 * INNER], F32, kind="ExternalInput")
    conv_w = p.dram("conv_w", [INNER, 3], F32, kind="ExternalInput")
    conv_b = p.dram("conv_b", [INNER], F32, kind="ExternalInput")
    W_dt = p.dram("W_dt", [INNER, DS], F32, kind="ExternalInput")
    b_dt = p.dram("b_dt", [DS], F32, kind="ExternalInput")
    W_B = p.dram("W_B", [INNER, DS], F32, kind="ExternalInput")
    b_B = p.dram("b_B", [DS], F32, kind="ExternalInput")
    W_C = p.dram("W_C", [INNER, DS], F32, kind="ExternalInput")
    b_C = p.dram("b_C", [DS], F32, kind="ExternalInput")
    W_si = p.dram("W_si", [DS, INNER], F32, kind="ExternalInput")
    b_si = p.dram("b_si", [INNER], F32, kind="ExternalInput")
    Dvec = p.dram("Dvec", [INNER], F32, kind="ExternalInput")
    W_out = p.dram("W_out", [INNER, DM], F32, kind="ExternalInput")
    b_out = p.dram("b_out", [DM], F32, kind="ExternalInput")
    carry_in = p.dram("carry_in", [DS, 1], F32, kind="ExternalInput")
    xz_halo = p.dram("xz_halo", [INNER, 2], F32, kind="ExternalInput")

    x_out = p.dram("x_out", [DM, T], F32, kind="ExternalOutput")
    state_out = p.dram("state_out", [DS, 1], F32, kind="ExternalOutput")
    xz_tail = p.dram("xz_tail", [INNER, 2], F32, kind="ExternalOutput")

    xm_sp = p.dram("xm_sp", [ICC, 128, T], BF16)
    gate_sp = p.dram("gate_sp", [ICC, 128, T], BF16)

    ln = LNState(p, "ln", inplace_xc=True)
    ones_col = p.sb("ones_col", [128, 1], BF16)
    ones_row = p.sb("ones_row", [1, 128], F32)
    wbig = p.sb("wbig", [128, 16 * DM], BF16)
    wdtbc = p.sb("wdtbc", [128, ICC, 192], BF16)
    wsi = p.sb("wsi", [DS, INNER], BF16)
    xtile = p.sb("xtile", [128, DC, TT], F32)
    ztile = p.sb("ztile", [128, DC, TT], BF16)
    exto = p.sb("exto", [128, 516], F32)
    convo = p.sb("convo", [128, 512], F32)
    xmo = p.sb("xmo", [128, 512], BF16)
    gato = p.sb("gato", [128, 512], BF16)
    cstate = p.sb("cstate", [128, ICC, 2], F32)
    dtb = p.sb("dtb", [64, T], F32)
    Bb = p.sb("Bb", [64, T], BF16)
    Cb = p.sb("Cb", [64, T], BF16)
    ub = p.sb("ub", [64, T], F32)
    ysb = p.sb("ysb", [64, T], F32)
    ynb = p.sb("ynb", [64, T], BF16)
    zero5 = p.sb("zero5", [64, TT], F32)
    stat_s = p.sb("stat_s", [1, TT], F32)
    stat_s2 = p.sb("stat_s2", [1, TT], F32)
    mub64 = p.sb("mub64", [64, TT], F32)
    y3t = p.sb("y3t", [128, ICC, TT], BF16)
    otile = p.sb("otile", [128, TT], F32)
    carry_c = p.sb("carry_c", [64, 1], F32)
    binc = p.sb("binc", [128, 32], F32)
    cwc = p.sb("cwc", [128, ICC, 3], F32)
    cbc = p.sb("cbc", [128, ICC], F32)
    bdtc = p.sb("bdtc", [128, 1], F32)
    bcc = p.sb("bcc", [64, 1], F32)
    bsic = p.sb("bsic", [128, ICC], F32)
    dvc = p.sb("dvc", [128, ICC], F32)
    boutc = p.sb("boutc", [128, DC], F32)

    pb = [p.psum(f"pb{i}", [128, 512], F32) for i in range(8)]

    def f1(eng):
        return eng.memset(ones_col.t[:, :], 1.0)
    p.step("vector", f1, writes=[ones_col], tag="ones_col")

    def f2(eng):
        return eng.memset(ones_row.t[:, :], 1.0)
    p.step("vector", f2, writes=[ones_row], tag="ones_row")

    def f3(eng):
        return eng.memset(zero5.t[:, :], 0.0)
    p.step("vector", f3, writes=[zero5], tag="zero5")

    wiv = wbig.t.ap().rearrange("p (kc o) -> p kc o", kc=DC) if False else None
    g_cast(p, wdtbc.t[:, :, 0:64], W_dt.t.rearrange("(kc p) o -> p kc o", p=128),
           [W_dt], [wdtbc], tag="wdt")
    g_cast(p, wdtbc.t[:, :, 64:128], W_B.t.rearrange("(kc p) o -> p kc o", p=128),
           [W_B], [wdtbc], tag="wB")
    g_cast(p, wdtbc.t[:, :, 128:192], W_C.t.rearrange("(kc p) o -> p kc o", p=128),
           [W_C], [wdtbc], tag="wC")
    g_cast(p, wsi.t[:, :], W_si.t[:, :], [W_si], [wsi], tag="wsi")

    s_dma(p, binc.t[:, :], b_in.t.rearrange("(kc p) -> p kc", p=128),
          [b_in], [binc], tag="binc", allow_slow=True)
    s_dma(p, cwc.t[:, :, :], conv_w.t.rearrange("(kc p) k -> p kc k", p=128),
          [conv_w], [cwc], tag="cwc", allow_slow=True)
    s_dma(p, cbc.t[:, :], conv_b.t.rearrange("(kc p) -> p kc", p=128),
          [conv_b], [cbc], tag="cbc", allow_slow=True)
    s_dma(p, bdtc.t[0:64, 0:1], b_dt.t.rearrange("(a p) -> p a", p=64),
          [b_dt], [bdtc], tag="bdtc", allow_slow=True)
    s_dma(p, bdtc.t[64:128, 0:1], b_B.t.rearrange("(a p) -> p a", p=64),
          [b_B], [bdtc], tag="bBc", allow_slow=True)
    s_dma(p, bcc.t[0:64, 0:1], b_C.t.rearrange("(a p) -> p a", p=64),
          [b_C], [bcc], tag="bCc", allow_slow=True)
    s_dma(p, bsic.t[:, :], b_si.t.rearrange("(kc p) -> p kc", p=128),
          [b_si], [bsic], tag="bsic", allow_slow=True)
    s_dma(p, dvc.t[:, :], Dvec.t.rearrange("(kc p) -> p kc", p=128),
          [Dvec], [dvc], tag="dvc", allow_slow=True)
    s_dma(p, boutc.t[:, :], b_out.t.rearrange("(kc p) -> p kc", p=128),
          [b_out], [boutc], tag="boutc", allow_slow=True)
    s_dma(p, carry_c.t[:, :], carry_in.t[:, :], [carry_in], [carry_c], tag="carry")
    g_cast(p, cstate.t[:, :, :], xz_halo.t.rearrange("(kc p) c -> p kc c", p=128),
           [xz_halo], [cstate], tag="halo")

    g_cast(p, wbig.t[:, :].rearrange("p (kc o) -> p kc o", kc=DC),
           W_in.t[:, INNER:2 * INNER].rearrange("(kc p) o -> p kc o", p=128),
           [W_in], [wbig], tag="wing")
    for tt in range(NTT):
        t0 = tt * TT

        def fxl(eng, dsem, t0=t0):
            eng.dma_start(out=xtile.t[:, :, :],
                          in_=x_in.t.rearrange("(kc p) t -> p kc t", p=128)[:, :, t0:t0 + TT]
                          ).then_inc(dsem, 16)
        p.dma("sync", fxl, 1, reads=[x_in], writes=[xtile], tag=f"G_xl{tt}")

        def xs_ap(kc):
            return xtile.t[:, kc, :]

        def z_ap(kc):
            return ztile.t[:, kc, :]

        emit_ln_tile(p, ln, xs_ap, xtile, z_ap, ztile, 0, pb[0], pb[1],
                     ones_col, ones_row, tag=f"G_ln{tt}_")

        for oc in range(ICC):
            pbo = pb[2 + (oc % 2)]

            def fmm(eng, oc=oc, pbo=pbo):
                last = None
                for kc in range(DC):
                    lhsT = wbig.t[:, kc * INNER + oc * 128:kc * INNER + oc * 128 + 128]
                    eng.ldweights(lhsT)
                    last = eng.matmul(pbo.t[0:128, 0:TT], lhsT, ztile.t[:, kc, :],
                                      start=(kc == 0), stop=(kc == DC - 1))
                return last
            p.step("tensor", fmm, reads=[wbig, ztile], writes=[pbo], tag=f"G_mm{tt}_{oc}")

            def fev(eng, oc=oc, pbo=pbo):
                return eng.tensor_scalar_add(gato.t[:, :], pbo.t[0:128, 0:TT],
                                             binc.t[:, 16 + oc:16 + oc + 1])
            p.step("vector", fev, reads=[pbo, binc], writes=[gato], tag=f"G_ev{tt}_{oc}")

            def fgs(eng, dsem, oc=oc, t0=t0):
                eng.dma_start(out=gate_sp.t[oc, :, t0:t0 + TT],
                              in_=gato.t[:, :]).then_inc(dsem, 16)
            p.dma("sync", fgs, 1, reads=[gato], writes=[gate_sp], tag=f"G_gs{tt}_{oc}")

    g_cast(p, wbig.t[:, :].rearrange("p (kc o) -> p kc o", kc=DC),
           W_in.t[:, 0:INNER].rearrange("(kc p) o -> p kc o", p=128),
           [W_in], [wbig], tag="winx")
    for tt in range(NTT):
        t0 = tt * TT

        def fxl(eng, dsem, t0=t0):
            eng.dma_start(out=xtile.t[:, :, :],
                          in_=x_in.t.rearrange("(kc p) t -> p kc t", p=128)[:, :, t0:t0 + TT]
                          ).then_inc(dsem, 16)
        p.dma("sync", fxl, 1, reads=[x_in], writes=[xtile], tag=f"Axl{tt}")

        def xs_ap(kc):
            return xtile.t[:, kc, :]

        def z_ap(kc):
            return ztile.t[:, kc, :]

        emit_ln_tile(p, ln, xs_ap, xtile, z_ap, ztile, 0, pb[0], pb[1],
                     ones_col, ones_row, tag=f"Aln{tt}_")

        for oc in range(ICC):
            pbo = pb[2 + (oc % 2)]

            def fmm(eng, oc=oc, pbo=pbo):
                last = None
                for kc in range(DC):
                    lhsT = wbig.t[:, kc * INNER + oc * 128:kc * INNER + oc * 128 + 128]
                    eng.ldweights(lhsT)
                    last = eng.matmul(pbo.t[0:128, 0:TT], lhsT, ztile.t[:, kc, :],
                                      start=(kc == 0), stop=(kc == DC - 1))
                return last
            p.step("tensor", fmm, reads=[wbig, ztile], writes=[pbo], tag=f"Amm{tt}_{oc}")

            def fst(eng, oc=oc):
                return eng.tensor_copy(exto.t[:, 0:2], cstate.t[:, oc, :])
            p.step("vector", fst, reads=[cstate], writes=[exto], tag=f"Ast{tt}_{oc}")

            def fev(eng, oc=oc, pbo=pbo):
                return eng.tensor_scalar_add(exto.t[:, 2:514], pbo.t[0:128, 0:TT],
                                             binc.t[:, oc:oc + 1])
            p.step("vector", fev, reads=[pbo, binc], writes=[exto], tag=f"Aev{tt}_{oc}")

            def fsv(eng, oc=oc):
                return eng.tensor_copy(cstate.t[:, oc, :], exto.t[:, 512:514])
            p.step("vector", fsv, reads=[exto], writes=[cstate], tag=f"Asv{tt}_{oc}")

            def fc0(eng, oc=oc):
                return eng.tensor_scalar_mul(convo.t[:, :], exto.t[:, 0:512],
                                             cwc.t[:, oc, 0:1])
            p.step("vector", fc0, reads=[exto, cwc], writes=[convo], tag=f"Ac0{tt}_{oc}")

            def fc1(eng, oc=oc):
                return eng.scalar_tensor_tensor(convo.t[:, :], exto.t[:, 1:513],
                                                cwc.t[:, oc, 1:2], convo.t[:, :],
                                                ALU.mult, ALU.add)
            p.step("vector", fc1, reads=[exto, cwc, convo], writes=[convo],
                   tag=f"Ac1{tt}_{oc}")

            def fc2(eng, oc=oc):
                return eng.scalar_tensor_tensor(convo.t[:, :], exto.t[:, 2:514],
                                                cwc.t[:, oc, 2:3], convo.t[:, :],
                                                ALU.mult, ALU.add)
            p.step("vector", fc2, reads=[exto, cwc, convo], writes=[convo],
                   tag=f"Ac2{tt}_{oc}")

            def fsl(eng, oc=oc):
                return eng.activation(out=xmo.t[:, :], in_=convo.t[:, :],
                                      func=AF.Silu, bias=cbc.t[:, oc:oc + 1])
            p.step("scalar", fsl, reads=[convo, cbc], writes=[xmo], tag=f"Asl{tt}_{oc}")

            def fxs(eng, dsem, oc=oc, t0=t0):
                eng.dma_start(out=xm_sp.t[oc, :, t0:t0 + TT],
                              in_=xmo.t[:, :]).then_inc(dsem, 16)
            p.dma("sync", fxs, 1, reads=[xmo], writes=[xm_sp], tag=f"Axs{tt}_{oc}")

            def fdt(eng, oc=oc):
                lhsT = wdtbc.t[:, oc, 0:128]
                eng.ldweights(lhsT)
                eng.matmul(pb[4].t[0:128, 0:TT], lhsT, xmo.t[:, :],
                           start=(oc == 0), stop=(oc == ICC - 1))
                lhsT2 = wdtbc.t[:, oc, 128:192]
                eng.ldweights(lhsT2)
                m2 = eng.matmul(pb[5].t[0:64, 0:TT], lhsT2, xmo.t[:, :],
                                start=(oc == 0), stop=(oc == ICC - 1))
                return m2
            p.step("tensor", fdt, reads=[wdtbc, xmo], writes=[pb[4], pb[5]],
                   tag=f"Adt{tt}_{oc}")

        def fdt2(eng, t0=t0):
            return eng.activation(out=dtb.t[:, t0:t0 + TT], in_=pb[4].t[0:64, 0:TT],
                                  func=AF.Sigmoid, bias=bdtc.t[0:64, 0:1])
        p.step("scalar", fdt2, reads=[pb[4], bdtc], writes=[dtb], tag=f"Adt2{tt}")

        def fB2(eng, t0=t0):
            return eng.activation(out=Bb.t[:, t0:t0 + TT], in_=pb[4].t[64:128, 0:TT],
                                  func=AF.Identity, bias=bdtc.t[64:128, 0:1])
        p.step("scalar", fB2, reads=[pb[4], bdtc], writes=[Bb], tag=f"AB2{tt}")

        def fC2(eng, t0=t0):
            return eng.activation(out=Cb.t[:, t0:t0 + TT], in_=pb[5].t[0:64, 0:TT],
                                  func=AF.Identity, bias=bcc.t[0:64, 0:1])
        p.step("scalar", fC2, reads=[pb[5], bcc], writes=[Cb], tag=f"AC2{tt}")

    g_cast(p, xz_tail.t.rearrange("(kc p) c -> p kc c", p=128), cstate.t[:, :, :],
           [cstate], [xz_tail], tag="tailout")

    def fu(eng):
        return eng.tensor_mul(ub.t[:, :], dtb.t[:, :], Bb.t[:, :])
    p.step("vector", fu, reads=[dtb, Bb], writes=[ub], tag="u")

    def fa(eng):
        return eng.tensor_scalar(dtb.t[:, :], dtb.t[:, :], -1.0, 1.0,
                                 ALU.mult, ALU.add)
    p.step("vector", fa, reads=[dtb], writes=[dtb], tag="a")

    def fys(eng):
        return eng.tensor_tensor_scan(out=ysb.t[:, :], data0=dtb.t[:, :],
                                      data1=ub.t[:, :], initial=0.0,
                                      op0=ALU.mult, op1=ALU.add)
    p.step("vector", fys, reads=[dtb, ub], writes=[ysb], tag="ys")

    for tt in range(NTT):
        t0 = tt * TT

        def fcp(eng, tt=tt, t0=t0):
            init = 1.0 if tt == 0 else ub.t[:, t0 - 1:t0]
            return eng.tensor_tensor_scan(out=ub.t[:, t0:t0 + TT],
                                          data0=dtb.t[:, t0:t0 + TT],
                                          data1=zero5.t[:, :], initial=init,
                                          op0=ALU.mult, op1=ALU.add)
        p.step("vector", fcp, reads=[dtb, zero5, ub], writes=[ub], tag=f"cp{tt}")

    def ffix(eng):
        return eng.scalar_tensor_tensor(ysb.t[:, :], ub.t[:, :], carry_c.t[:, 0:1],
                                        ysb.t[:, :], ALU.mult, ALU.add)
    p.step("vector", ffix, reads=[ub, carry_c, ysb], writes=[ysb], tag="fix")

    def fso(eng, dsem):
        eng.dma_start(out=state_out.t[:, :], in_=ysb.t[:, T - 1:T]).then_inc(dsem, 16)
    p.dma("sync", fso, 1, reads=[ysb], writes=[state_out], tag="stateout")

    def fy(eng):
        return eng.tensor_mul(ysb.t[:, :], Cb.t[:, :], ysb.t[:, :])
    p.step("vector", fy, reads=[Cb, ysb], writes=[ysb], tag="y")

    for tt in range(NTT):
        t0 = tt * TT

        def fyb(eng, t0=t0):
            return eng.copy(ynb.t[:, t0:t0 + TT], ysb.t[:, t0:t0 + TT])
        p.step("scalar", fyb, reads=[ysb], writes=[ynb], tag=f"Yc{tt}")

        def fsum(eng, t0=t0):
            eng.ldweights(ones_col.t[0:64, 0:1])
            return eng.matmul(pb[0].t[0:1, 0:TT], ones_col.t[0:64, 0:1],
                              ynb.t[:, t0:t0 + TT], start=True, stop=True)
        p.step("tensor", fsum, reads=[ynb, ones_col], writes=[pb[0]], tag=f"Ys{tt}")

        def fmu(eng):
            return eng.mul(stat_s.t[0:1, :], pb[0].t[0:1, 0:TT], 1.0 / DS)
        p.step("scalar", fmu, reads=[pb[0]], writes=[stat_s], tag=f"Ym{tt}")

        def fbc(eng):
            return eng.matmul(pb[1].t[0:64, 0:TT], ones_row.t[0:1, 0:64],
                              stat_s.t[0:1, :], start=True, stop=True)
        p.step("tensor", fbc, reads=[stat_s, ones_row], writes=[pb[1]], tag=f"Yb{tt}")

        def fmub(eng):
            return eng.tensor_copy(mub64.t[:, :], pb[1].t[0:64, 0:TT])
        p.step("vector", fmub, reads=[pb[1]], writes=[mub64], tag=f"Ybe{tt}")

        def fyc(eng, t0=t0):
            return eng.tensor_sub(ysb.t[:, t0:t0 + TT], ysb.t[:, t0:t0 + TT],
                                  mub64.t[:, :])
        p.step("vector", fyc, reads=[ysb, mub64], writes=[ysb], tag=f"Yx{tt}")

        def fsq(eng, t0=t0):
            return eng.activation(out=ynb.t[:, t0:t0 + TT], in_=ysb.t[:, t0:t0 + TT],
                                  func=AF.Square)
        p.step("scalar", fsq, reads=[ysb], writes=[ynb], tag=f"Yq{tt}")

        def fss(eng, t0=t0):
            eng.ldweights(ones_col.t[0:64, 0:1])
            return eng.matmul(pb[0].t[0:1, 0:TT], ones_col.t[0:64, 0:1],
                              ynb.t[:, t0:t0 + TT], start=True, stop=True)
        p.step("tensor", fss, reads=[ynb, ones_col], writes=[pb[0]], tag=f"Yss{tt}")

        def fstd(eng):
            return eng.activation(out=stat_s2.t[0:1, :], in_=pb[0].t[0:1, 0:TT],
                                  func=AF.Sqrt, scale=1.0 / DS,
                                  bias=ln.eps.t[0:1, 0:1])
        p.step("scalar", fstd, reads=[pb[0], ln.eps], writes=[stat_s2], tag=f"Yst{tt}")

        def frcp(eng):
            return eng.reciprocal(stat_s2.t[0:1, :], stat_s2.t[0:1, :])
        p.step("vector", frcp, reads=[stat_s2], writes=[stat_s2], tag=f"Yr{tt}")

        def fbc2(eng):
            return eng.matmul(pb[1].t[0:64, 0:TT], ones_row.t[0:1, 0:64],
                              stat_s2.t[0:1, :], start=True, stop=True)
        p.step("tensor", fbc2, reads=[stat_s2, ones_row], writes=[pb[1]], tag=f"Yb2{tt}")

        def fsde(eng):
            return eng.tensor_copy(mub64.t[:, :], pb[1].t[0:64, 0:TT])
        p.step("vector", fsde, reads=[pb[1]], writes=[mub64], tag=f"Yse{tt}")

        def fyn(eng, t0=t0):
            return eng.tensor_mul(ynb.t[:, t0:t0 + TT], ysb.t[:, t0:t0 + TT],
                                  mub64.t[:, :])
        p.step("vector", fyn, reads=[ysb, mub64], writes=[ynb], tag=f"Yn{tt}")

    wout_v = Buf(wbig.t, "wout_v", (128, 16 * DM), BF16)
    wout_v.r = wbig.r
    g_cast(p, wbig.t[:, :].rearrange("p (kc o) -> p kc o", kc=ICC),
           W_out.t.rearrange("(kc p) o -> p kc o", p=128),
           [W_out], [wout_v], tag="woutld")

    for tt in range(NTT):
        t0 = tt * TT
        for oc in range(ICC):
            pbo = pb[2 + (oc % 2)]

            def fy2(eng, oc=oc, pbo=pbo, t0=t0):
                lhsT = wsi.t[0:64, oc * 128:(oc + 1) * 128]
                eng.ldweights(lhsT)
                return eng.matmul(pbo.t[0:128, 0:TT], lhsT, ynb.t[:, t0:t0 + TT],
                                  start=True, stop=True)
            p.step("tensor", fy2, reads=[wsi, ynb], writes=[pbo], tag=f"Cy2{tt}_{oc}")

            def fxm(eng, dsem, oc=oc, t0=t0):
                eng.dma_start(out=xmo.t[:, :],
                              in_=xm_sp.t[oc, :, t0:t0 + TT]).then_inc(dsem, 16)
            p.dma("sync", fxm, 1, reads=[xm_sp], writes=[xmo], tag=f"Cxm{tt}_{oc}")

            def fga(eng, dsem, oc=oc, t0=t0):
                eng.dma_start(out=gato.t[:, :],
                              in_=gate_sp.t[oc, :, t0:t0 + TT]).then_inc(dsem, 16)
            p.dma("sync", fga, 1, reads=[gate_sp], writes=[gato], tag=f"Cga{tt}_{oc}")

            def fsg(eng, oc=oc):
                return eng.activation(out=gato.t[:, :], in_=gato.t[:, :],
                                      func=AF.Sigmoid)
            p.step("scalar", fsg, reads=[gato], writes=[gato], tag=f"Csg{tt}_{oc}")

            def ft1(eng, oc=oc, pbo=pbo):
                return eng.scalar_tensor_tensor(convo.t[:, :], xmo.t[:, :],
                                                dvc.t[:, oc:oc + 1], pbo.t[0:128, 0:TT],
                                                ALU.mult, ALU.add)
            p.step("vector", ft1, reads=[xmo, dvc, pbo], writes=[convo], tag=f"Ct1{tt}_{oc}")

            def ft2(eng, oc=oc):
                return eng.tensor_scalar_add(convo.t[:, :], convo.t[:, :],
                                             bsic.t[:, oc:oc + 1])
            p.step("vector", ft2, reads=[convo, bsic], writes=[convo], tag=f"Ct2{tt}_{oc}")

            def ft3(eng, oc=oc):
                return eng.tensor_mul(y3t.t[:, oc, :], convo.t[:, :], gato.t[:, :])
            p.step("vector", ft3, reads=[convo, gato], writes=[y3t], tag=f"Ct3{tt}_{oc}")

        def fxl(eng, dsem, t0=t0):
            eng.dma_start(out=xtile.t[:, :, :],
                          in_=x_in.t.rearrange("(kc p) t -> p kc t", p=128)[:, :, t0:t0 + TT]
                          ).then_inc(dsem, 16)
        p.dma("sync", fxl, 1, reads=[x_in], writes=[xtile], tag=f"Cxl{tt}")

        for oc in range(DC):
            pbo = pb[4 + (oc % 2)]

            def fmo(eng, oc=oc, pbo=pbo):
                last = None
                for kc in range(ICC):
                    lhsT = wbig.t[:, kc * DM + oc * 128:kc * DM + oc * 128 + 128]
                    eng.ldweights(lhsT)
                    last = eng.matmul(pbo.t[0:128, 0:TT], lhsT, y3t.t[:, kc, :],
                                      start=(kc == 0), stop=(kc == ICC - 1))
                return last
            p.step("tensor", fmo, reads=[wout_v, y3t], writes=[pbo], tag=f"Cmo{tt}_{oc}")

            def fep(eng, oc=oc, pbo=pbo):
                return eng.activation(out=otile.t[:, 0:TT], in_=pbo.t[0:128, 0:TT],
                                      func=AF.Identity, bias=boutc.t[:, oc:oc + 1])
            p.step("scalar", fep, reads=[pbo, boutc], writes=[otile], tag=f"Cep{tt}_{oc}")

            def fadd(eng, oc=oc, t0=t0):
                return eng.tensor_add(xtile.t[:, oc, :], xtile.t[:, oc, :],
                                      otile.t[:, 0:TT])
            p.step("vector", fadd, reads=[otile, xtile], writes=[xtile],
                   tag=f"Cadd{tt}_{oc}")

        def fxs(eng, dsem, t0=t0):
            eng.dma_start(out=x_out.t.rearrange("(kc p) t -> p kc t", p=128)[:, :, t0:t0 + TT],
                          in_=xtile.t[:, :, :]).then_inc(dsem, 16)
        p.dma("sync", fxs, 1, reads=[xtile], writes=[x_out], tag=f"Cxs{tt}")

    PARANOID[0] = old_par
    p.emit()
    return nc


_L1ATTN_NC = None


def get_l1attn():
    global _L1ATTN_NC
    if _L1ATTN_NC is None:
        _L1ATTN_NC = build_l1attn()
    return _L1ATTN_NC


def run_l1attn(x_shards, P):
    nc = get_l1attn()
    zero_carry = np.zeros((64, 1), np.float32)
    zero_halo = np.zeros((2048, 2), np.float32)

    def maps(carries, halos):
        ms = []
        for c in range(NCORES):
            ms.append({
                "x_in": x_shards[c],
                "W_in": P["L1_attn_W_in"], "b_in": P["L1_attn_b_in"],
                "conv_w": P["L1_attn_conv_w"], "conv_b": P["L1_attn_conv_b"],
                "W_dt": P["L1_attn_W_dt"], "b_dt": P["L1_attn_b_dt"],
                "W_B": P["L1_attn_W_B"], "b_B": P["L1_attn_b_B"],
                "W_C": P["L1_attn_W_C"], "b_C": P["L1_attn_b_C"],
                "W_si": P["L1_attn_W_si"], "b_si": P["L1_attn_b_si"],
                "Dvec": P["L1_attn_D"],
                "W_out": P["L1_attn_W_out"], "b_out": P["L1_attn_b_out"],
                "carry_in": carries[c], "xz_halo": halos[c],
            })
        return ms

    res1 = run_bass_kernel_spmd(nc, maps([zero_carry] * 8, [zero_halo] * 8),
                                core_ids=list(range(NCORES)))
    carries = []
    halos = []
    for c in range(NCORES):
        if c % 2 == 0:
            carries.append(zero_carry)
            halos.append(zero_halo)
        else:
            carries.append(np.asarray(res1.results[c - 1]["state_out"]))
            halos.append(np.asarray(res1.results[c - 1]["xz_tail"]))
    res2 = run_bass_kernel_spmd(nc, maps(carries, halos),
                                core_ids=list(range(NCORES)))
    out = []
    for c in range(NCORES):
        r = res1 if c % 2 == 0 else res2
        out.append(np.asarray(r.results[c]["x_out"]))
    return out




NH = 32
DHD = 32
MM = 2
CHK = 128
NCH = T // CHK
NSLAB = 8
EPS_DN = 1e-6


def build_l0attn(phases="1234", p3depth=99, dbg=False):
    nc = bass.Bass("TRN2", target_bir_lowering=False, debug=False, detect_race_conditions=False)
    p = Prog(nc)

    x_in = p.dram("x_in", [DM, T], F32, kind="ExternalInput")
    Wq = p.dram("Wq", [DM, DM], F32, kind="ExternalInput")
    Wk = p.dram("Wk", [DM, DM], F32, kind="ExternalInput")
    Wv = p.dram("Wv", [DM, DM], F32, kind="ExternalInput")
    Wbeta = p.dram("Wbeta", [64, DM], F32, kind="ExternalInput")
    bbeta = p.dram("bbeta", [64], F32, kind="ExternalInput")
    Wmix = p.dram("Wmix", [64, DM], F32, kind="ExternalInput")
    bmix = p.dram("bmix", [64], F32, kind="ExternalInput")
    Wout = p.dram("Wout", [DM, DM], F32, kind="ExternalInput")
    bout = p.dram("bout", [DM], F32, kind="ExternalInput")
    ln_g = p.dram("ln_g", [DM], F32, kind="ExternalInput")
    ln_b = p.dram("ln_b", [DM], F32, kind="ExternalInput")
    ident_d = p.dram("c_ident", [128, 128], F32, kind="ExternalInput")
    ropec = p.dram("c_ropecos", [T, 16], F32, kind="ExternalInput")
    ropes = p.dram("c_ropesin", [T, 16], F32, kind="ExternalInput")
    maskA = p.dram("c_maskA", [128, 512], F32, kind="ExternalInput")
    EmC = p.dram("c_Em", [NSLAB * MM, 64, 128], F32, kind="ExternalInput")
    state_in = p.dram("state_in", [NSLAB, 128, 66], F32, kind="ExternalInput")

    x_out = p.dram("x_out", [DM, T], F32, kind="ExternalOutput")
    state_out = p.dram("state_out", [NSLAB, 128, 66], F32, kind="ExternalOutput")

    spk = "ExternalOutput" if dbg else "Internal"
    phiq_sp = p.dram("phiq_sp", [T, DM], BF16, kind=spk)
    phik_sp = p.dram("phik_sp", [T, DM], BF16, kind=spk)
    vaug_sp = p.dram("vaug_sp", [T, NH * 33], BF16, kind=spk)
    y_sp = p.dram("y_sp", [T, DM], BF16, kind=spk)
    stats_sp = p.dram("stats_sp", [2, T], F32, kind=spk)
    if dbg:
        dbgT = p.dram("dbgT", [4, 128, 512], F32, kind="ExternalOutput")
        dbgN = p.dram("dbgN", [4, 128, 264], F32, kind="ExternalOutput")
        dbgF = p.dram("dbgF", [4, 128, 8], F32, kind="ExternalOutput")
        dbgP = p.dram("dbgP", [128, NCH, 64], F32, kind="ExternalOutput")
        dbgC = p.dram("dbgC", [128, NCH, MM], F32, kind="ExternalOutput")
        dbgQ = p.dram("dbgQ", [128, 512], F32, kind="ExternalOutput")
        dbgPL = p.dram("dbgPL", [64, NCH], F32, kind="ExternalOutput")
        dbgE = p.dram("dbgE", [64, 128], F32, kind="ExternalOutput")
        dbgR = p.dram("dbgR", [4, 128, 1024], F32, kind="ExternalOutput")
        dbgV = p.dram("dbgV", [4, 128, 132], F32, kind="ExternalOutput")
        dbgD = p.dram("dbgD", [4, 128, 24], F32, kind="ExternalOutput")

    ln = LNState(p, "ln", inplace_xc=True)
    ones_col = p.sb("ones_col", [128, 1], BF16)
    ones_row = p.sb("ones_row", [1, 128], F32)
    identb = p.sb("identb", [128, 128], BF16)
    identf = p.sb("identf", [128, 128], F32)
    maskb = p.sb("maskb", [128, 512], BF16)
    wstage = p.sb("wstage", [128, DM], BF16)
    wbmT = p.sb("wbmT", [128, DC, 128], BF16)
    bbc = p.sb("bbc", [64, 2], F32)
    qraw = p.sb("qraw", [128, DM], BF16)
    qrot = p.sb("qrot", [128, DM], BF16)
    krot = p.sb("krot", [128, DM], BF16)
    phit = p.sb("phit", [128, DM], BF16)
    vtmp = p.sb("vtmp", [128, NH * 33], BF16)
    ropecb = p.sb("ropecb", [128, 16], BF16)
    ropesb = p.sb("ropesb", [128, 16], BF16)
    mlog = p.sb("mlog", [64, TT], F32)
    mrow2 = p.sb("mrow2", [32, TT], F32)
    zeroC = p.sb("zeroC", [64, CHK], F32)
    ptm = p.sb("ptm", [128, NCH, 64], BF16)
    rcptm = p.sb("rcptm", [128, NCH, 64], BF16)
    sptm = p.sb("sptm", [128, NCH, 64], BF16)
    pmtm = p.sb("pmtm", [128, NCH, 64], F32)
    PLm = p.sb("PLm", [64, NCH], F32)
    PLC = p.sb("PLC", [128, NCH, MM], F32)
    EmS = p.sb("EmS", [64, MM, 128], F32)
    Sall = p.sb("Sall", [128, 66], F32)
    Sb16 = p.sb("Sb16", [128, 66], BF16)
    Tbase = p.sb("Tbase", [128, 512], BF16)
    Traw = p.sb("Traw", [128, MM, 512], BF16)
    numb = p.sb("numb", [128, MM, 132], F32)
    dent = p.sb("dent", [128, MM, 4], F32)
    ffac = p.sb("ffac", [128, MM, 4], F32)
    vscl = p.sb("vscl", [128, MM, 132], BF16)
    ytmp = p.sb("ytmp", [128, 128], F32)
    ytmp2 = p.sb("ytmp2", [128, 128], F32)
    dencp = p.sb("dencp", [128, 24], F32)
    gcols = p.sb("gcols", [128, DC], F32)
    bcols = p.sb("bcols", [128, DC], F32)
    boutc = p.sb("boutc", [128, DC], F32)
    mrow = p.sb("mrow", [1, TT], F32)
    srow = p.sb("srow", [1, TT], F32)
    zfm = p.sb("zfm", [128, DC, T], BF16)

    pbS = [p.psum(f"pbs{i}", [128, 512], F32) for i in range(2)]
    pbA = p.psum("pbA", [128, 2048], F32)
    pq = []
    for i in range(4):
        b = Buf(pbA.t, f"pq{i}", (128, 512), F32)
        pq.append(b)
    pbt = [p.psum(f"pbt{i}", [128, 128], BF16) for i in range(2)]

    def qA(i, r0, r1, c0, c1):
        return pbA.t[r0:r1, i * 512 + c0:i * 512 + c1]

    AR = p.arena_reserve("arena", 73728)
    xtile1 = p.sb_at("a_xtile1", [128, DC, TT], F32, AR + 0)
    beta_f = p.sb_at("a_beta", [64, T], F32, AR + 0)
    mix_f = p.sb_at("a_mix", [64, T], BF16, AR + 16384)
    p_f = p.sb_at("a_p", [64, T], F32, AR + 24576)
    rcp_f = p.sb_at("a_rcp", [64, T], F32, AR + 40960)
    sp_f = p.sb_at("a_sp", [64, T], F32, AR + 57344)
    wq3 = p.sb_at("a_wq", [128, DC, DM], BF16, AR + 0)
    wk3 = p.sb_at("a_wk", [128, DC, DM], BF16, AR + 16384)
    wv3 = p.sb_at("a_wv", [128, DC, DM], BF16, AR + 32768)
    phiq_f = p.sb_at("a_phq", [128, T], BF16, AR + 0)
    phik_f = p.sb_at("a_phk", [128, T], BF16, AR + 8192)
    phik_t = p.sb_at("a_pkt", [128, NCH, 128], BF16, AR + 16384)
    vaug_t = p.sb_at("a_vat", [128, NCH, 132], BF16, AR + 24576)
    yacc = p.sb_at("a_yac", [128, NCH, 128], BF16, AR + 33024)
    woT = p.sb_at("a_wo", [128, DC, DM], BF16, AR + 0)
    utile = p.sb_at("a_ut", [128, DC, TT], F32, AR + 16384)
    xtile4 = p.sb_at("a_xt4", [128, DC, TT], F32, AR + 32768)
    xc2 = p.sb_at("a_xc2", [128, DC, TT], F32, AR + 49152)
    mub2 = p.sb_at("a_mub2", [128, TT], F32, AR + 65536)
    rsdb2 = p.sb_at("a_rsd2", [128, TT], F32, AR + 67584)

    def f1(eng):
        return eng.memset(ones_col.t[:, :], 1.0)
    p.step("vector", f1, writes=[ones_col], tag="ones_col")

    def f2(eng):
        return eng.memset(ones_row.t[:, :], 1.0)
    p.step("vector", f2, writes=[ones_row], tag="ones_row")

    def f3(eng):
        return eng.memset(zeroC.t[:, :], 0.0)
    p.step("vector", f3, writes=[zeroC], tag="zeroC")

    g_cast(p, identb.t[:, :], ident_d.t[:], [ident_d], [identb], tag="identb")
    s_dma(p, identf.t[:, :], ident_d.t[:], [ident_d], [identf], tag="identf")
    g_cast(p, maskb.t[:, :], maskA.t[:, :], [maskA], [maskb], tag="maskb")

    for (wsrc, coff, tg) in ((Wbeta, 0, "wb"), (Wmix, 64, "wm")):
        for m in range(2):
            g_cast(p, wstage.t[m * 32:(m + 1) * 32, 0:DM],
                   wsrc.t.rearrange("(h m) d -> h m d", m=2)[:, m, :],
                   [wsrc], [wstage], tag=tg + f"ld{m}")
        for kc in range(DC):
            pbx = pbt[kc % 2]

            def ftr(eng, kc=kc, pbx=pbx):
                return eng.transpose(pbx.t[0:128, 0:64],
                                     wstage.t[0:64, kc * 128:(kc + 1) * 128],
                                     identb.t[0:64, 0:64])
            p.step("tensor", ftr, reads=[wstage, identb], writes=[pbx],
                   tag=f"{tg}tr{kc}")

            def fev(eng, kc=kc, pbx=pbx, coff=coff):
                return eng.tensor_copy(wbmT.t[:, kc, coff:coff + 64],
                                       pbx.t[0:128, 0:64])
            p.step("vector", fev, reads=[pbx], writes=[wbmT], tag=f"{tg}ev{kc}")

    for m in range(2):
        s_dma(p, bbc.t[m * 32:(m + 1) * 32, 0:1],
              bbeta.t.rearrange("(h m) -> h m", m=2)[:, m:m + 1],
              [bbeta], [bbc], tag=f"bbc{m}", allow_slow=True)
        s_dma(p, bbc.t[m * 32:(m + 1) * 32, 1:2],
              bmix.t.rearrange("(h m) -> h m", m=2)[:, m:m + 1],
              [bmix], [bbc], tag=f"bmc{m}", allow_slow=True)

    for tt in range(NTT):
        t0 = tt * TT

        def fxl(eng, dsem, t0=t0):
            eng.dma_start(out=xtile1.t[:, :, :],
                          in_=x_in.t.rearrange("(kc p) t -> p kc t", p=128)[:, :, t0:t0 + TT]
                          ).then_inc(dsem, 16)
        p.dma("sync", fxl, 1, reads=[x_in], writes=[xtile1], tag=f"P1x{tt}")

        def xs_ap(kc):
            return xtile1.t[:, kc, :]

        def z_ap(kc, t0=t0):
            return zfm.t[:, kc, t0:t0 + TT]

        emit_ln_tile(p, ln, xs_ap, xtile1, z_ap, zfm, 0, pbS[0], pbS[1],
                     ones_col, ones_row, tag=f"P1l{tt}_")

        def fst(eng, dsem, t0=t0):
            eng.dma_start(out=stats_sp.t[0:1, t0:t0 + TT],
                          in_=ln.mu.t[0:1, 0:TT]).then_inc(dsem, 16)
            eng.dma_start(out=stats_sp.t[1:2, t0:t0 + TT],
                          in_=ln.rstd.t[0:1, 0:TT]).then_inc(dsem, 16)
        p.dma("sync", fst, 2, reads=[ln.mu, ln.rstd], writes=[stats_sp],
              tag=f"P1s{tt}")

    if "2" not in phases:
        p.emit()
        return nc
    p.handoff([xtile1], [beta_f, mix_f, p_f, rcp_f, sp_f])
    for tt in range(NTT):
        t0 = tt * TT

        def fbm(eng, t0=t0):
            last = None
            for kc in range(DC):
                eng.ldweights(wbmT.t[:, kc, 0:128])
                last = eng.matmul(qA(2, 0, 128, 0, TT), wbmT.t[:, kc, 0:128],
                                  zfm.t[:, kc, t0:t0 + TT],
                                  start=(kc == 0), stop=(kc == DC - 1))
            return last
        p.step("tensor", fbm, reads=[wbmT, zfm], writes=[pq[2]], tag=f"P2bm{tt}")

        def fbe(eng, t0=t0):
            return eng.activation(out=beta_f.t[:, t0:t0 + TT], in_=qA(2, 0, 64, 0, TT),
                                  func=AF.Sigmoid, bias=bbc.t[0:64, 0:1])
        p.step("scalar", fbe, reads=[pq[2], bbc], writes=[beta_f], tag=f"P2be{tt}")

        def fbc2(eng, t0=t0):
            return eng.tensor_scalar(beta_f.t[:, t0:t0 + TT], beta_f.t[:, t0:t0 + TT],
                                     0.85, 0.9995, ALU.max, ALU.min)
        p.step("vector", fbc2, reads=[beta_f], writes=[beta_f], tag=f"P2bc{tt}")

        def fme(eng, t0=t0):
            return eng.activation(out=mlog.t[0:64, 0:TT], in_=qA(2, 64, 128, 0, TT),
                                  func=AF.Identity, bias=bbc.t[0:64, 1:2])
        p.step("scalar", fme, reads=[pq[2], bbc], writes=[mlog], tag=f"P2me{tt}")

        def fmd(eng, t0=t0):
            eng.tensor_copy(mrow2.t[0:32, 0:TT], mlog.t[32:64, 0:TT])
            return eng.tensor_sub(mlog.t[0:32, 0:TT], mlog.t[0:32, 0:TT],
                                  mrow2.t[0:32, 0:TT])
        p.step("vector", fmd, reads=[mlog, mrow2], writes=[mlog, mrow2], tag=f"P2md{tt}")

        def fms(eng, t0=t0):
            return eng.activation(out=mix_f.t[0:32, t0:t0 + TT], in_=mlog.t[0:32, 0:TT],
                                  func=AF.Sigmoid)
        p.step("scalar", fms, reads=[mlog], writes=[mix_f], tag=f"P2ms{tt}")

        def fm1(eng, t0=t0):
            return eng.tensor_scalar(mix_f.t[32:64, t0:t0 + TT], mix_f.t[0:32, t0:t0 + TT],
                                     -1.0, 1.0, ALU.mult, ALU.add)
        p.step("vector", fm1, reads=[mix_f], writes=[mix_f], tag=f"P2m1{tt}")

    for c in range(NCH):
        def fcp(eng, c=c):
            return eng.tensor_tensor_scan(out=p_f.t[:, c * CHK:(c + 1) * CHK],
                                          data0=beta_f.t[:, c * CHK:(c + 1) * CHK],
                                          data1=zeroC.t[:, :], initial=1.0,
                                          op0=ALU.mult, op1=ALU.add)
        p.step("vector", fcp, reads=[beta_f, zeroC], writes=[p_f], tag=f"pcp{c}")

    def frc(eng):
        return eng.reciprocal(rcp_f.t[:, :], p_f.t[:, :])
    p.step("vector", frc, reads=[p_f], writes=[rcp_f], tag="rcp")

    for c in range(NCH):
        def fsp(eng, c=c):
            return eng.tensor_scalar_mul(sp_f.t[:, c * CHK:(c + 1) * CHK],
                                         rcp_f.t[:, c * CHK:(c + 1) * CHK],
                                         p_f.t[:, c * CHK + CHK - 1:c * CHK + CHK])
        p.step("vector", fsp, reads=[rcp_f, p_f], writes=[sp_f], tag=f"sp{c}")

    def fplm(eng):
        return eng.tensor_copy(PLm.t[:, :],
                               p_f.t[:, :].rearrange("q (c k) -> q c k", k=CHK)[:, :, CHK - 1])
    p.step("vector", fplm, reads=[p_f], writes=[PLm], tag="plm")

    for (srcb, dstb, tg) in ((p_f, ptm, "tp"), (rcp_f, rcptm, "tr"),
                             (sp_f, sptm, "ts"), (mix_f, pmtm, "tm")):
        for c in range(NCH):
            bi = c % 2
            if srcb.dtype == F32:
                def ftr(eng, srcb=srcb, c=c, bi=bi):
                    return eng.transpose(qA(bi, 0, 128, 0, 64),
                                         srcb.t[:, c * CHK:(c + 1) * CHK],
                                         identf.t[0:64, 0:64])
                p.step("tensor", ftr, reads=[srcb, identf], writes=[pq[bi]],
                       tag=f"{tg}t{c}")

                def fev(eng, dstb=dstb, c=c, bi=bi):
                    return eng.tensor_copy(dstb.t[:, c, :], qA(bi, 0, 128, 0, 64))
                p.step("vector", fev, reads=[pq[bi]], writes=[dstb], tag=f"{tg}e{c}")
            else:
                pbx = pbt[c % 2]

                def ftr(eng, srcb=srcb, c=c, pbx=pbx):
                    return eng.transpose(pbx.t[0:128, 0:64],
                                         srcb.t[:, c * CHK:(c + 1) * CHK],
                                         identb.t[0:64, 0:64])
                p.step("tensor", ftr, reads=[srcb, identb], writes=[pbx],
                       tag=f"{tg}t{c}")

                def fev(eng, dstb=dstb, c=c, pbx=pbx):
                    return eng.tensor_copy(dstb.t[:, c, :], pbx.t[0:128, 0:64])
                p.step("vector", fev, reads=[pbx], writes=[dstb], tag=f"{tg}e{c}")

    def fpm(eng):
        return eng.tensor_mul(pmtm.t[:, :, :], pmtm.t[:, :, :], ptm.t[:, :, :])
    p.step("vector", fpm, reads=[pmtm, ptm], writes=[pmtm], tag="pm")

    p.handoff([beta_f, mix_f, p_f, rcp_f, sp_f], [wq3, wk3, wv3])
    prep_weight_T(p, wq3, Wq, identb, pbt, DM, DM, wstage, "wq")
    prep_weight_T(p, wk3, Wk, identb, pbt, DM, DM, wstage, "wk")
    prep_weight_T(p, wv3, Wv, identb, pbt, DM, DM, wstage, "wv")

    for tc in range(T // 128):
        c0 = tc * 128

        def frt(eng, dsem, c0=c0):
            eng.dma_start(out=ropecb.t[:, :], in_=ropec.t[c0:c0 + 128, :]).then_inc(dsem, 16)
            eng.dma_start(out=ropesb.t[:, :], in_=ropes.t[c0:c0 + 128, :]).then_inc(dsem, 16)
        p.dma("gpsimd", frt, 2, reads=[ropec, ropes], writes=[ropecb, ropesb],
              tag=f"P2r{tc}")

        for (wbuf, dstb, tg) in ((wq3, qrot, "q"), (wk3, krot, "k")):
            for half in range(2):
                def fmm(eng, wbuf=wbuf, half=half, c0=c0):
                    last = None
                    for kc in range(DC):
                        eng.ldweights(zfm.t[:, kc, c0:c0 + 128])
                        last = eng.matmul(qA(half, 0, 128, 0, TT),
                                          zfm.t[:, kc, c0:c0 + 128],
                                          wbuf.t[:, kc, half * 512:(half + 1) * 512],
                                          start=(kc == 0), stop=(kc == DC - 1))
                    return last
                p.step("tensor", fmm, reads=[zfm, wbuf], writes=[pq[half]],
                       tag=f"P2{tg}mm{tc}_{half}")

                def fev(eng, half=half):
                    return eng.tensor_copy(qraw.t[:, half * 512:(half + 1) * 512],
                                           qA(half, 0, 128, 0, TT))
                p.step("vector", fev, reads=[pq[half]], writes=[qraw], tag=f"P2{tg}ev{tc}_{half}")

            def fr1(eng, dstb=dstb):
                s = qraw.t[:, 0:DM].rearrange("p (h two f) -> p h two f", two=2, f=16)
                d = dstb.t[:, 0:DM].rearrange("p (h two f) -> p h two f", two=2, f=16)
                ph = phit.t[:, 0:DM].rearrange("p (h two f) -> p h two f", two=2, f=16)
                cb = ropecb.t[:, :].unsqueeze(1).broadcast_to([128, 32, 16])
                sb = ropesb.t[:, :].unsqueeze(1).broadcast_to([128, 32, 16])
                eng.tensor_tensor(d[:, :, 0, :], s[:, :, 0, :], cb, ALU.mult)
                eng.tensor_tensor(d[:, :, 1, :], s[:, :, 0, :], sb, ALU.mult)
                eng.tensor_tensor(ph[:, :, 0, :], s[:, :, 1, :], sb, ALU.mult)
                eng.tensor_tensor(ph[:, :, 1, :], s[:, :, 1, :], cb, ALU.mult)
                eng.tensor_sub(d[:, :, 0, :], d[:, :, 0, :], ph[:, :, 0, :])
                return eng.tensor_add(d[:, :, 1, :], d[:, :, 1, :], ph[:, :, 1, :])
            p.step("vector", fr1, reads=[qraw, ropecb, ropesb],
                   writes=[dstb, phit], tag=f"P2r{tg}{tc}")

            def fp1(eng, dstb=dstb):
                return eng.tensor_scalar_min(phit.t[:, :], dstb.t[:, :], 0.0)
            p.step("vector", fp1, reads=[dstb], writes=[phit], tag=f"P2p{tg}a{tc}")

            def fp2(eng):
                return eng.activation(out=phit.t[:, :], in_=phit.t[:, :], func=AF.Exp)
            p.step("scalar", fp2, reads=[phit], writes=[phit], tag=f"P2p{tg}b{tc}")

            def fp3(eng, dstb=dstb):
                return eng.tensor_scalar_max(dstb.t[:, :], dstb.t[:, :], 0.0)
            p.step("vector", fp3, reads=[dstb], writes=[dstb], tag=f"P2p{tg}c{tc}")

            def fp4(eng, dstb=dstb):
                return eng.tensor_add(dstb.t[:, :], dstb.t[:, :], phit.t[:, :])
            p.step("vector", fp4, reads=[dstb, phit], writes=[dstb], tag=f"P2p{tg}d{tc}")

        def fsq(eng, dsem, c0=c0):
            eng.dma_start(out=phiq_sp.t[c0:c0 + 128, :], in_=qrot.t[:, :]).then_inc(dsem, 16)
            eng.dma_start(out=phik_sp.t[c0:c0 + 128, :], in_=krot.t[:, :]).then_inc(dsem, 16)
        p.dma("sync", fsq, 2, reads=[qrot, krot], writes=[phiq_sp, phik_sp],
              tag=f"P2sp{tc}")

        for half in range(2):
            def fmm(eng, half=half, c0=c0):
                last = None
                for kc in range(DC):
                    eng.ldweights(zfm.t[:, kc, c0:c0 + 128])
                    last = eng.matmul(qA(half, 0, 128, 0, TT),
                                      zfm.t[:, kc, c0:c0 + 128],
                                      wv3.t[:, kc, half * 512:(half + 1) * 512],
                                      start=(kc == 0), stop=(kc == DC - 1))
                return last
            p.step("tensor", fmm, reads=[zfm, wv3], writes=[pq[half]], tag=f"P2vmm{tc}_{half}")

            def fev(eng, half=half):
                dst = vtmp.t[:, :].rearrange("p (h c2) -> p h c2", c2=33)[:, half * 16:(half + 1) * 16, 0:32]
                return eng.tensor_copy(dst, qA(half, 0, 128, 0, TT).rearrange("p (h d) -> p h d", d=32))
            p.step("vector", fev, reads=[pq[half]], writes=[vtmp], tag=f"P2vev{tc}_{half}")

        def fvo(eng):
            return eng.memset(vtmp.t[:, :].rearrange("p (h c2) -> p h c2", c2=33)[:, :, 32:33], 1.0)
        p.step("vector", fvo, writes=[vtmp], tag=f"P2vo{tc}")

        def fvs(eng, dsem, c0=c0):
            eng.dma_start(out=vaug_sp.t[c0:c0 + 128, :], in_=vtmp.t[:, :]).then_inc(dsem, 16)
        p.dma("sync", fvs, 1, reads=[vtmp], writes=[vaug_sp], tag=f"P2vs{tc}")

    if "3" not in phases:
        p.emit()
        return nc
    p.handoff([wq3, wk3, wv3], [phiq_f, phik_f, phik_t, vaug_t, yacc])
    for s in range(NSLAB):
        if True:
            def fq(eng, dsem, s=s):
                eng.dma_start(out=phiq_f.t[:, :], in_=phiq_sp.t[:, s * 128:(s + 1) * 128],
                              transpose=True).then_inc(dsem, 16)
                eng.dma_start(out=phik_f.t[:, :], in_=phik_sp.t[:, s * 128:(s + 1) * 128],
                              transpose=True).then_inc(dsem, 16)
            p.dma("sync", fq, 2, reads=[phiq_sp, phik_sp], writes=[phiq_f, phik_f],
                  tag=f"S{s}ld")

        if True:
            def fkt(eng, dsem, s=s):
                eng.dma_start(out=phik_t.t[:, :, :],
                              in_=phik_sp.t[:, s * 128:(s + 1) * 128].rearrange(
                                  "(c p) d -> p c d", p=128)).then_inc(dsem, 16)
                eng.dma_start(out=vaug_t.t[:, :, :],
                              in_=vaug_sp.t[:, s * 132:(s + 1) * 132].rearrange(
                                  "(c p) d -> p c d", p=128)).then_inc(dsem, 16)
            p.dma("sync", fkt, 2, reads=[phik_sp, vaug_sp], writes=[phik_t, vaug_t],
                  tag=f"S{s}ld2")

        if True:
            def fsi(eng, dsem, s=s):
                eng.dma_start(out=Sall.t[:, :], in_=state_in.t[s, :, :]).then_inc(dsem, 16)
            p.dma("sync", fsi, 1, reads=[state_in], writes=[Sall], tag=f"S{s}si")

        if True:
            def fem(eng, dsem, s=s):
                eng.dma_start(out=EmS.t[:, 0, :], in_=EmC.t[s * 2 + 0, :, :]).then_inc(dsem, 16)
                eng.dma_start(out=EmS.t[:, 1, :], in_=EmC.t[s * 2 + 1, :, :]).then_inc(dsem, 16)
            p.dma("sync", fem, 2, reads=[EmC], writes=[EmS], tag=f"S{s}em")

        if p3depth < 1:
            continue

        if dbg and s == 0:
            def fdbg3(eng, dsem):
                eng.dma_start(out=dbgQ.t[:, :], in_=phiq_f.t[:, 0:512]).then_inc(dsem, 16)
                eng.dma_start(out=dbgPL.t[:, :], in_=PLm.t[:, :]).then_inc(dsem, 16)
                eng.dma_start(out=dbgE.t[:, :], in_=EmS.t[:, 0, :]).then_inc(dsem, 16)
            p.dma("gpsimd", fdbg3, 3, reads=[phiq_f, PLm, EmS],
                  writes=[dbgQ, dbgPL, dbgE], tag="dbg3")

        def fplc(eng, s=s):
            last = None
            for m in range(MM):
                last = eng.matmul(qA(3, 0, 128, m * NCH, (m + 1) * NCH),
                                  EmS.t[:, m, :], PLm.t[:, :], start=True, stop=True)
            return last
        p.step("tensor", fplc, reads=[EmS, PLm], writes=[pq[3]], tag=f"S{s}plc")

        def fple(eng):
            return eng.tensor_copy(PLC.t[:, :, :],
                                   qA(3, 0, 128, 0, 2 * NCH).rearrange("p (m c) -> p c m", m=MM))
        p.step("vector", fple, reads=[pq[3]], writes=[PLC], tag=f"S{s}ple")

        for c in range(NCH):
            def fSc(eng):
                return eng.tensor_copy(Sb16.t[:, :], Sall.t[:, :])
            p.step("vector", fSc, reads=[Sall], writes=[Sb16], tag=f"S{s}Sc{c}")

            if p3depth < 2:
                continue

            def fA(eng, c=c):
                last = None
                for h in range(4):
                    lhsT = phik_f.t[h * 32:(h + 1) * 32, c * CHK:(c + 1) * CHK]
                    rhs = phiq_f.t[h * 32:(h + 1) * 32, c * CHK:(c + 1) * CHK]
                    eng.ldweights(lhsT, tile_position=(32 * h, 0))
                    last = eng.matmul(qA(h, 0, 128, 0, 128), lhsT, rhs,
                                      start=True, stop=True, tile_position=(32 * h, 0))
                return last
            p.step("tensor", fA, reads=[phik_f, phiq_f],
                   writes=[pq[0], pq[1], pq[2], pq[3]], tag=f"S{s}A{c}")

            if p3depth < 3:
                continue

            def fTb(eng):
                pv = pbA.t[0:128, 0:2048].rearrange("p (h w) -> p h w", w=512)[:, :, 0:128]
                mv = maskb.t[:, :].rearrange("p (h i) -> p h i", h=4)
                return eng.tensor_tensor(Tbase.t[:, :].rearrange("p (h i) -> p h i", h=4),
                                         pv, mv, ALU.mult)
            p.step("vector", fTb, reads=[pq[0], pq[1], pq[2], pq[3], maskb],
                   writes=[Tbase], tag=f"S{s}Tb{c}")

            def fTr(eng, s=s, c=c):
                tb = Tbase.t[:, :].unsqueeze(1).broadcast_to([128, MM, 512]
                     ).rearrange("p m (h i) -> p m h i", h=4)
                rc = rcptm.t[:, c, :].rearrange("p (m h) -> p m h", m=MM)[:, :, s * 4:(s + 1) * 4]
                rcb = rc.unsqueeze(3).broadcast_to([128, MM, 4, 128])
                return eng.tensor_tensor(
                    Traw.t[:, :, :].rearrange("p m (h i) -> p m h i", h=4),
                    tb, rcb, ALU.mult)
            p.step("vector", fTr, reads=[Tbase, rcptm], writes=[Traw], tag=f"S{s}Tr{c}")

            if p3depth < 4:
                continue

            def fN(eng, s=s, c=c):
                last = None
                for m in range(MM):
                    for h in range(4):
                        lhsT = Traw.t[:, m, h * 128:(h + 1) * 128]
                        eng.ldweights(lhsT)
                        eng.matmul(qA(m, 0, 128, h * 33, (h + 1) * 33), lhsT,
                                   vaug_t.t[:, c, h * 33:(h + 1) * 33],
                                   start=True, stop=False)
                        lhsT2 = phiq_f.t[h * 32:(h + 1) * 32, c * CHK:(c + 1) * CHK]
                        eng.ldweights(lhsT2, tile_position=(32 * h, 0))
                        last = eng.matmul(qA(m, 0, 128, h * 33, (h + 1) * 33), lhsT2,
                                          Sb16.t[h * 32:(h + 1) * 32, m * 33:(m + 1) * 33],
                                          start=False, stop=True,
                                          tile_position=(32 * h, 0))
                return last
            p.step("tensor", fN, reads=[Traw, vaug_t, phiq_f, Sb16],
                   writes=[pq[0], pq[1]], tag=f"S{s}N{c}")

            def fNe(eng):
                eng.tensor_copy(numb.t[:, 0, :], qA(0, 0, 128, 0, 132))
                return eng.tensor_copy(numb.t[:, 1, :], qA(1, 0, 128, 0, 132))
            p.step("vector", fNe, reads=[pq[0], pq[1]], writes=[numb], tag=f"S{s}Ne{c}")

            if p3depth < 5:
                continue

            def fD1(eng, s=s, c=c):
                den = numb.t[:, :, :].rearrange("p m (h c2) -> p m h c2", c2=33)[:, :, :, 32]
                pv = ptm.t[:, c, :].rearrange("p (m h) -> p m h", m=MM)[:, :, s * 4:(s + 1) * 4]
                return eng.tensor_tensor(dent.t[:, :, :], den, pv, ALU.mult)
            p.step("vector", fD1, reads=[numb, ptm], writes=[dent], tag=f"S{s}D1{c}")

            def fD2(eng):
                return eng.tensor_scalar_add(dent.t[:, :, :], dent.t[:, :, :], EPS_DN)
            p.step("vector", fD2, reads=[dent], writes=[dent], tag=f"S{s}D2{c}")

            def fF1(eng):
                return eng.reciprocal(ffac.t[:, :, :], dent.t[:, :, :])
            p.step("vector", fF1, reads=[dent], writes=[ffac], tag=f"S{s}F1{c}")

            def fF2(eng, s=s, c=c):
                pm = pmtm.t[:, c, :].rearrange("p (m h) -> p m h", m=MM)[:, :, s * 4:(s + 1) * 4]
                return eng.tensor_mul(ffac.t[:, :, :], ffac.t[:, :, :], pm)
            p.step("vector", fF2, reads=[ffac, pmtm], writes=[ffac], tag=f"S{s}F2{c}")

            if dbg and s == 0 and c < 4:
                def fden(eng, c=c, s=s):
                    den = numb.t[:, :, :].rearrange("p m (h c2) -> p m h c2", c2=33)[:, :, :, 32]
                    eng.tensor_copy(dencp.t[:, 0:8].rearrange("p (m h) -> p m h", m=MM), den)
                    pv = ptm.t[:, c, :].rearrange("p (m h) -> p m h", m=MM)[:, :, s * 4:(s + 1) * 4]
                    eng.tensor_copy(dencp.t[:, 8:16].rearrange("p (m h) -> p m h", m=MM), pv)
                    return eng.tensor_copy(dencp.t[:, 16:24].rearrange("p (m h) -> p m h", m=MM),
                                           dent.t[:, :, :])
                p.step("vector", fden, reads=[numb, ptm, dent], writes=[dencp],
                       tag=f"dencp{c}")

                def fdbgD(eng, dsem, c=c):
                    eng.dma_start(out=dbgD.t[c, :, :], in_=dencp.t[:, :]).then_inc(dsem, 16)
                p.dma("gpsimd", fdbgD, 1, reads=[dencp], writes=[dbgD], tag=f"dbgD{c}")

                def fdbgR(eng, dsem, c=c):
                    eng.dma_start(out=dbgR.t[c, :, :], in_=Traw.t[:, :, :]).then_inc(dsem, 16)
                    eng.dma_start(out=dbgV.t[c, :, :], in_=vaug_t.t[:, c, :]).then_inc(dsem, 16)
                    eng.dma_start(out=dbgT.t[c, :, :], in_=Tbase.t[:, :]).then_inc(dsem, 16)
                    eng.dma_start(out=dbgN.t[c, :, :], in_=numb.t[:, :, :]).then_inc(dsem, 16)
                    eng.dma_start(out=dbgF.t[c, :, :], in_=ffac.t[:, :, :]).then_inc(dsem, 16)
                p.dma("gpsimd", fdbgR, 5, reads=[Traw, vaug_t, Tbase, numb, ffac],
                      writes=[dbgR, dbgV, dbgT, dbgN, dbgF], tag=f"dbg{c}")
            if dbg and s == 0 and c == 0:
                def fdbg2(eng, dsem):
                    eng.dma_start(out=dbgP.t[:, :, :], in_=ptm.t[:, :, :]).then_inc(dsem, 16)
                    eng.dma_start(out=dbgC.t[:, :, :], in_=PLC.t[:, :, :]).then_inc(dsem, 16)
                p.dma("gpsimd", fdbg2, 2, reads=[ptm, PLC], writes=[dbgP, dbgC],
                      tag="dbg2")

            def fY1(eng, c=c):
                nv = numb.t[:, :, :].rearrange("p m (h c2) -> p m h c2", c2=33)[:, :, :, 0:32]
                fb = ffac.t[:, :, :].unsqueeze(3).broadcast_to([128, MM, 4, 32])
                return eng.tensor_tensor(ytmp.t[:, :].rearrange("p (h d) -> p h d", d=32),
                                         nv[:, 0], fb[:, 0], ALU.mult)
            p.step("vector", fY1, reads=[numb, ffac], writes=[ytmp], tag=f"S{s}Y1{c}")

            def fY2(eng, c=c):
                nv = numb.t[:, :, :].rearrange("p m (h c2) -> p m h c2", c2=33)[:, :, :, 0:32]
                fb = ffac.t[:, :, :].unsqueeze(3).broadcast_to([128, MM, 4, 32])
                return eng.tensor_tensor(ytmp2.t[:, :].rearrange("p (h d) -> p h d", d=32),
                                         nv[:, 1], fb[:, 1], ALU.mult)
            p.step("vector", fY2, reads=[numb, ffac], writes=[ytmp2], tag=f"S{s}Y2{c}")

            def fY3(eng, c=c):
                return eng.tensor_add(yacc.t[:, c, :], ytmp.t[:, :], ytmp2.t[:, :])
            p.step("vector", fY3, reads=[ytmp, ytmp2], writes=[yacc], tag=f"S{s}Y3{c}")

            def fV(eng, s=s, c=c):
                va = vaug_t.t[:, c, :].unsqueeze(1).broadcast_to([128, MM, 132]
                     ).rearrange("p m (h c2) -> p m h c2", c2=33)
                spv = sptm.t[:, c, :].rearrange("p (m h) -> p m h", m=MM)[:, :, s * 4:(s + 1) * 4]
                spb = spv.unsqueeze(3).broadcast_to([128, MM, 4, 33])
                return eng.tensor_tensor(
                    vscl.t[:, :, :].rearrange("p m (h c2) -> p m h c2", c2=33),
                    va, spb, ALU.mult)
            p.step("vector", fV, reads=[vaug_t, sptm], writes=[vscl], tag=f"S{s}V{c}")

            if p3depth < 6:
                continue

            def fB(eng, c=c):
                last = None
                for h in range(4):
                    lhsT = phik_t.t[:, c, h * 32:(h + 1) * 32]
                    eng.ldweights(lhsT)
                    for m in range(MM):
                        last = eng.matmul(qA(2, 0, 32, (h * 2 + m) * 33, (h * 2 + m + 1) * 33),
                                          lhsT, vscl.t[:, m, h * 33:(h + 1) * 33],
                                          start=True, stop=True)
                return last
            p.step("tensor", fB, reads=[phik_t, vscl], writes=[pq[2]], tag=f"S{s}B{c}")

            def fS(eng, s=s, c=c):
                last = None
                for h in range(4):
                    for m in range(MM):
                        last = eng.scalar_tensor_tensor(
                            Sall.t[h * 32:(h + 1) * 32, m * 33:(m + 1) * 33],
                            Sall.t[h * 32:(h + 1) * 32, m * 33:(m + 1) * 33],
                            PLC.t[h * 32:(h + 1) * 32, c, m:m + 1],
                            qA(2, 0, 32, (h * 2 + m) * 33, (h * 2 + m + 1) * 33),
                            ALU.mult, ALU.add)
                return last
            p.step("vector", fS, reads=[Sall, PLC, pq[2]], writes=[Sall], tag=f"S{s}S{c}")

        def fys(eng, dsem, s=s):
            eng.dma_start(out=y_sp.t[:, s * 128:(s + 1) * 128].rearrange(
                "(c p) d -> p c d", p=128), in_=yacc.t[:, :, :]).then_inc(dsem, 16)
        p.dma("sync", fys, 1, reads=[yacc], writes=[y_sp], tag=f"S{s}ys")

        def fso(eng, dsem, s=s):
            eng.dma_start(out=state_out.t[s, :, :], in_=Sall.t[:, :]).then_inc(dsem, 16)
        p.dma("sync", fso, 1, reads=[Sall], writes=[state_out], tag=f"S{s}so")

    if "4" not in phases:
        p.emit()
        return nc
    p.handoff([phiq_f, phik_f, phik_t, vaug_t, yacc],
              [woT, utile, xtile4, xc2, mub2, rsdb2])
    prep_weight_T(p, woT, Wout, identb, pbt, DM, DM, wstage, "wo")
    load_bias_cols(p, gcols, ln_g, DM, "lgc")
    load_bias_cols(p, bcols, ln_b, DM, "lbc")
    load_bias_cols(p, boutc, bout, DM, "boc")

    yfm = Buf(zfm.t, "yfm", (128, DC, T), BF16)
    yfm.r = zfm.r
    for kc in range(DC):
        def fyt(eng, dsem, kc=kc):
            eng.dma_start(out=zfm.t[:, kc, :], in_=y_sp.t[:, kc * 128:(kc + 1) * 128],
                          transpose=True).then_inc(dsem, 16)
        p.dma("sync", fyt, 1, reads=[y_sp], writes=[yfm], tag=f"P4yt{kc}")

    for tt in range(NTT):
        t0 = tt * TT

        def fxl(eng, dsem, t0=t0):
            eng.dma_start(out=xtile4.t[:, :, :],
                          in_=x_in.t.rearrange("(kc p) t -> p kc t", p=128)[:, :, t0:t0 + TT]
                          ).then_inc(dsem, 16)
        p.dma("sync", fxl, 1, reads=[x_in], writes=[xtile4], tag=f"P4x{tt}")

        def fmr(eng, dsem, t0=t0):
            eng.dma_start(out=mrow.t[0:1, :], in_=stats_sp.t[0:1, t0:t0 + TT]).then_inc(dsem, 16)
            eng.dma_start(out=srow.t[0:1, :], in_=stats_sp.t[1:2, t0:t0 + TT]).then_inc(dsem, 16)
        p.dma("gpsimd", fmr, 2, reads=[stats_sp], writes=[mrow, srow], tag=f"P4s{tt}")

        def fbc(eng, t0=t0):
            eng.matmul(pbS[0].t[0:128, 0:TT], ones_row.t[0:1, :], mrow.t[0:1, :],
                       start=True, stop=True)
            return eng.matmul(pbS[1].t[0:128, 0:TT], ones_row.t[0:1, :], srow.t[0:1, :],
                              start=True, stop=True)
        p.step("tensor", fbc, reads=[mrow, srow, ones_row], writes=[pbS[0], pbS[1]],
               tag=f"P4bc{tt}")

        def fmbe(eng):
            eng.tensor_copy(mub2.t[:, :], pbS[0].t[0:128, 0:TT])
            return eng.tensor_copy(rsdb2.t[:, :], pbS[1].t[0:128, 0:TT])
        p.step("vector", fmbe, reads=[pbS[0], pbS[1]], writes=[mub2, rsdb2],
               tag=f"P4be{tt}")

        for oc in range(DC):
            bi = oc % 2

            def fmm(eng, oc=oc, t0=t0, bi=bi):
                last = None
                for kc in range(DC):
                    lhsT = woT.t[:, kc, oc * 128:(oc + 1) * 128]
                    eng.ldweights(lhsT)
                    last = eng.matmul(qA(bi, 0, 128, 0, TT), lhsT,
                                      zfm.t[:, kc, t0:t0 + TT],
                                      start=(kc == 0), stop=(kc == DC - 1))
                return last
            p.step("tensor", fmm, reads=[woT, yfm], writes=[pq[bi]], tag=f"P4mm{tt}_{oc}")

            def fu1(eng, oc=oc, bi=bi):
                eng.tensor_sub(utile.t[:, oc, :], xtile4.t[:, oc, :], mub2.t[:, :])
                eng.tensor_mul(utile.t[:, oc, :], utile.t[:, oc, :], rsdb2.t[:, :])
                eng.tensor_add(utile.t[:, oc, :], utile.t[:, oc, :], qA(bi, 0, 128, 0, TT))
                return eng.tensor_scalar_add(utile.t[:, oc, :], utile.t[:, oc, :],
                                             boutc.t[:, oc:oc + 1])
            p.step("vector", fu1, reads=[xtile4, mub2, rsdb2, pq[bi], boutc],
                   writes=[utile], tag=f"P4u{tt}_{oc}")

        def us_ap(kc):
            return utile.t[:, kc, :]

        def o_ap(kc):
            return xc2.t[:, kc, :]

        emit_ln_tile(p, ln, us_ap, utile, o_ap, xc2, 0, pbS[0], pbS[1],
                     ones_col, ones_row, g_cols=gcols, b_cols=bcols,
                     tag=f"P4l{tt}_", xc_buf=xc2)

        def fadd(eng, t0=t0):
            last = None
            for kc in range(DC):
                last = eng.tensor_add(xtile4.t[:, kc, :], xtile4.t[:, kc, :],
                                      xc2.t[:, kc, :])
            return last
        p.step("vector", fadd, reads=[xtile4, xc2], writes=[xtile4], tag=f"P4a{tt}")

        def fxs(eng, dsem, t0=t0):
            eng.dma_start(out=x_out.t.rearrange("(kc p) t -> p kc t", p=128)[:, :, t0:t0 + TT],
                          in_=xtile4.t[:, :, :]).then_inc(dsem, 16)
        p.dma("sync", fxs, 1, reads=[xtile4], writes=[x_out], tag=f"P4xs{tt}")

    p.emit()
    return nc


_L0ATTN_NC = None


def get_l0attn():
    global _L0ATTN_NC
    if _L0ATTN_NC is None:
        _L0ATTN_NC = build_l0attn()
    return _L0ATTN_NC


def l0attn_consts(half):
    pos0 = half * T
    t = np.arange(pos0, pos0 + T, dtype=np.float32)
    inv_freq = 1.0 / 10000.0 ** (np.arange(16, dtype=np.float32) / 16.0)
    fr = t[:, None] * inv_freq[None, :]
    c = {}
    c["c_ropecos"] = np.cos(fr).astype(np.float32)
    c["c_ropesin"] = np.sin(fr).astype(np.float32)
    mask1 = np.triu(np.ones((128, 128), np.float32))
    c["c_maskA"] = np.tile(mask1, (1, 4)).astype(np.float32)
    Em = np.zeros((NSLAB * MM, 64, 128), np.float32)
    for s in range(NSLAB):
        for m in range(MM):
            for pp in range(128):
                Em[s * 2 + m, m * 32 + s * 4 + pp // 32, pp] = 1.0
    c["c_Em"] = Em
    c["c_ident"] = np.eye(128, dtype=np.float32)
    return c


def run_l0attn(x_shards, P):
    nc = get_l0attn()
    zstate = np.zeros((NSLAB, 128, 66), np.float32)
    consts = [l0attn_consts(c % 2) for c in range(NCORES)]

    def maps(states):
        ms = []
        for c in range(NCORES):
            ms.append({
                "x_in": x_shards[c],
                "Wq": P["L0_attn_Wq"], "Wk": P["L0_attn_Wk"], "Wv": P["L0_attn_Wv"],
                "Wbeta": P["L0_attn_Wbeta"], "bbeta": P["L0_attn_bbeta"],
                "Wmix": P["L0_attn_Wmix"], "bmix": P["L0_attn_bmix"],
                "Wout": P["L0_attn_Wout"], "bout": P["L0_attn_bout"],
                "ln_g": P["L0_attn_ln_g"], "ln_b": P["L0_attn_ln_b"],
                "state_in": states[c],
                **consts[c],
            })
        return ms

    res1 = run_bass_kernel_spmd(nc, maps([zstate] * NCORES),
                                core_ids=list(range(NCORES)))
    states = []
    for c in range(NCORES):
        if c % 2 == 0:
            states.append(zstate)
        else:
            states.append(np.asarray(res1.results[c - 1]["state_out"]))
    res2 = run_bass_kernel_spmd(nc, maps(states), core_ids=list(range(NCORES)))
    out = []
    for c in range(NCORES):
        r = res1 if c % 2 == 0 else res2
        out.append(np.asarray(r.results[c]["x_out"]))
    return out




HEADS = 8
DHEAD = 64
WIN = 512
INNER5 = 512
HID = 2730
NKC = T // 128 + 4


def build_l1f_begin():
    nc = bass.Bass("TRN2", target_bir_lowering=False, debug=False, detect_race_conditions=False)
    p = Prog(nc)
    x_in = p.dram("x_in", [DM, T], F32, kind="ExternalInput")
    z_out = p.dram("z_out", [DM, T], F32, kind="ExternalOutput")
    ones_col = p.sb("ones_col", [128, 1], BF16)
    ones_row = p.sb("ones_row", [1, 128], F32)
    xtile = p.sb("xtile", [128, DC, TT], F32)
    ztile = p.sb("ztile", [128, DC, TT], F32)
    ln = LNState(p, "ln", inplace_xc=False)
    pb = [p.psum(f"pb{i}", [128, 512], F32) for i in range(2)]

    def f1(eng):
        return eng.memset(ones_col.t[:, :], 1.0)
    p.step("vector", f1, writes=[ones_col], tag="oc")

    def f2(eng):
        return eng.memset(ones_row.t[:, :], 1.0)
    p.step("vector", f2, writes=[ones_row], tag="or")

    for tt in range(NTT):
        t0 = tt * TT

        def fxl(eng, dsem, t0=t0):
            eng.dma_start(out=xtile.t[:, :, :],
                          in_=x_in.t.rearrange("(kc p) t -> p kc t", p=128)[:, :, t0:t0 + TT]
                          ).then_inc(dsem, 16)
        p.dma("sync", fxl, 1, reads=[x_in], writes=[xtile], tag=f"xl{tt}")

        def xs_ap(kc):
            return xtile.t[:, kc, :]

        def z_ap(kc):
            return ztile.t[:, kc, :]

        emit_ln_tile(p, ln, xs_ap, xtile, z_ap, ztile, 0, pb[0], pb[1],
                     ones_col, ones_row, tag=f"l{tt}_")

        def fzs(eng, dsem, t0=t0):
            eng.dma_start(out=z_out.t.rearrange("(kc p) t -> p kc t", p=128)[:, :, t0:t0 + TT],
                          in_=ztile.t[:, :, :]).then_inc(dsem, 16)
        p.dma("sync", fzs, 1, reads=[ztile], writes=[z_out], tag=f"zs{tt}")

    p.emit()
    return nc


def build_l1f_end():
    nc = bass.Bass("TRN2", target_bir_lowering=False, debug=False, detect_race_conditions=False)
    p = Prog(nc)
    x_in = p.dram("x_in", [DM, T], F32, kind="ExternalInput")
    z_in = p.dram("z_in", [DM, T], F32, kind="ExternalInput")
    x_out = p.dram("x_out", [DM, T], F32, kind="ExternalOutput")
    xtile = p.sb("xtile", [128, DC, TT], F32)
    ztile = p.sb("ztile", [128, DC, TT], F32)
    half = p.sb("half", [128, 1], F32)

    def fh(eng):
        return eng.memset(half.t[:, :], 0.5)
    p.step("vector", fh, writes=[half], tag="half")

    for tt in range(NTT):
        t0 = tt * TT

        def fxl(eng, dsem, t0=t0):
            eng.dma_start(out=xtile.t[:, :, :],
                          in_=x_in.t.rearrange("(kc p) t -> p kc t", p=128)[:, :, t0:t0 + TT]
                          ).then_inc(dsem, 16)
            eng.dma_start(out=ztile.t[:, :, :],
                          in_=z_in.t.rearrange("(kc p) t -> p kc t", p=128)[:, :, t0:t0 + TT]
                          ).then_inc(dsem, 16)
        p.dma("sync", fxl, 2, reads=[x_in, z_in], writes=[xtile, ztile], tag=f"xl{tt}")

        def fadd(eng):
            last = None
            for kc in range(DC):
                last = eng.scalar_tensor_tensor(xtile.t[:, kc, :], ztile.t[:, kc, :],
                                                half.t[:, 0:1], xtile.t[:, kc, :],
                                                ALU.mult, ALU.add)
            return last
        p.step("vector", fadd, reads=[xtile, ztile, half], writes=[xtile], tag=f"ad{tt}")

        def fxs(eng, dsem, t0=t0):
            eng.dma_start(out=x_out.t.rearrange("(kc p) t -> p kc t", p=128)[:, :, t0:t0 + TT],
                          in_=xtile.t[:, :, :]).then_inc(dsem, 16)
        p.dma("sync", fxs, 1, reads=[xtile], writes=[x_out], tag=f"xs{tt}")

    p.emit()
    return nc


def build_l1f_attn():
    nc = bass.Bass("TRN2", target_bir_lowering=False, debug=False, detect_race_conditions=False)
    p = Prog(nc)

    z_in = p.dram("z_in", [DM, 1 + T], F32, kind="ExternalInput")
    zleft = p.dram("zleft", [DM, 513], F32, kind="ExternalInput")
    wqkv = p.dram("wqkv", [DM, 3 * INNER5], F32, kind="ExternalInput")
    qsc = p.dram("qsc", [DHEAD], F32, kind="ExternalInput")
    ksc = p.dram("ksc", [DHEAD], F32, kind="ExternalInput")
    wo = p.dram("wo", [INNER5, DM], F32, kind="ExternalInput")
    csq = p.dram("c_csq", [DHEAD, T], F32, kind="ExternalInput")
    ssq = p.dram("c_ssq", [DHEAD, T], F32, kind="ExternalInput")
    csk = p.dram("c_csk", [DHEAD, 512 + T], F32, kind="ExternalInput")
    ssk = p.dram("c_ssk", [DHEAD, 512 + T], F32, kind="ExternalInput")
    mL_i = p.dram("c_mLi", [128, 128], F32, kind="ExternalInput")
    mR_i = p.dram("c_mRi", [128, 128], F32, kind="ExternalInput")
    mL_j = p.dram("c_mLj", [128, 128], F32, kind="ExternalInput")
    mR_j = p.dram("c_mRj", [128, 128], F32, kind="ExternalInput")
    ident_d = p.dram("c_ident", [128, 128], F32, kind="ExternalInput")
    kbias = p.dram("kbias", [2, 512 + T], F32, kind="ExternalInput")
    z_out = p.dram("z_out", [DM, T], F32, kind="ExternalOutput")
    o_sp = p.dram("o_sp", [INNER5, T], BF16)

    TK = 512 + T

    zb = p.sb("zb", [128, DC, 1 + T], BF16)
    zlb = p.sb("zlb", [128, DC, 513], BF16)
    wqp = p.sb("wqp", [128, DC, 3, 128], BF16)
    wob = p.sb("wob", [128, 4, DM], BF16)
    ARF = p.arena_reserve("arf", 26624)
    qf = p.sb_at("a_qf", [128, T], BF16, ARF + 0)
    kf = p.sb_at("a_kf", [128, TK], BF16, ARF + 8192)
    vf = p.sb_at("a_vf", [128, TK], BF16, ARF + 17408)
    vtm = p.sb("vtm", [128, NKC, 130], BF16)
    Qa = p.sb("Qa", [66, T], BF16)
    Ka = p.sb("Ka", [66, TK], BF16)
    Pt = p.sb("Pt", [128, 6, 640], BF16)
    csqb = p.sb("csqb", [64, T], BF16)
    ssqb = p.sb("ssqb", [64, T], BF16)
    cskb = p.sb("cskb", [64, TK], BF16)
    sskb = p.sb("sskb", [64, TK], BF16)
    kbb = p.sb("kbb", [2, TK], BF16)
    onesr = p.sb("onesr", [1, 128], F32)
    onesc = p.sb("onesc", [128, 1], BF16)
    identb = p.sb("identb", [128, 128], BF16)
    identf = p.sb("identf", [128, 128], F32)
    qscb = p.sb("qscb", [64, 2], F32)
    sq64 = p.sb("sq64", [64, TT], BF16)
    nrm = p.sb("nrm", [1, TT], F32)
    nrmb = p.sb("nrmb", [64, TT], F32)
    t64 = p.sb("t64", [64, TT], F32)
    r64 = p.sb("r64", [64, TT], BF16)
    rsw = p.sb("rsw", [64, TT], BF16)
    mLi = p.sb("mLi", [128, 128], F32)
    mRi = p.sb("mRi", [128, 128], F32)
    mLj = p.sb("mLj", [128, 128], BF16)
    mRj = p.sb("mRj", [128, 128], BF16)
    negm = p.sb("negm", [128, 2], F32)
    denr = p.sb("denr", [1, 128], F32)
    obuf = p.sb("obuf", [65, 128], F32)
    obuf2 = p.sb("obuf2", [64, 128], BF16)
    ztile = p.sb_at("a_zt", [128, DC, TT], F32, ARF + 0)
    oftile = p.sb_at("a_of", [128, 4, TT], BF16, ARF + 16384)

    pbq = [p.psum(f"pbq{i}", [128, 512], F32) for i in range(2)]
    pbw = p.psum("pbw", [128, 640], F32)
    pbs = p.psum("pbs", [128, 640], F32)
    pbo1 = p.psum("pbo1", [128, 512], F32)
    pbtv = p.psum("pbtv", [128, 128], BF16)

    def f1(eng):
        return eng.memset(onesr.t[:, :], 1.0)
    p.step("vector", f1, writes=[onesr], tag="onesr")

    def f1b(eng):
        return eng.memset(onesc.t[:, :], 1.0)
    p.step("vector", f1b, writes=[onesc], tag="onesc")

    def f1c(eng):
        return eng.memset(negm.t[:, 0:1], 1.0)
    p.step("vector", f1c, writes=[negm], tag="negm1")

    g_cast(p, zb.t[:, :, :], z_in.t.rearrange("(kc p) t -> p kc t", p=128),
           [z_in], [zb], tag="zb")
    g_cast(p, zlb.t[:, :, :], zleft.t.rearrange("(kc p) t -> p kc t", p=128),
           [zleft], [zlb], tag="zlb")
    g_cast(p, wob.t[:, :, :], wo.t.rearrange("(kc p) o -> p kc o", p=128),
           [wo], [wob], tag="wob")
    g_cast(p, csqb.t[:, :], csq.t[:, :], [csq], [csqb], tag="csq")
    g_cast(p, ssqb.t[:, :], ssq.t[:, :], [ssq], [ssqb], tag="ssq")
    g_cast(p, cskb.t[:, :], csk.t[:, :], [csk], [cskb], tag="csk")
    g_cast(p, sskb.t[:, :], ssk.t[:, :], [ssk], [sskb], tag="ssk")
    g_cast(p, kbb.t[:, :], kbias.t[:, :], [kbias], [kbb], tag="kbb")
    g_cast(p, identb.t[:, :], ident_d.t[:, :], [ident_d], [identb], tag="idb")
    s_dma(p, identf.t[:, :], ident_d.t[:, :], [ident_d], [identf], tag="idf")
    s_dma(p, mLi.t[:, :], mL_i.t[:, :], [mL_i], [mLi], tag="mLi")
    s_dma(p, mRi.t[:, :], mR_i.t[:, :], [mR_i], [mRi], tag="mRi")
    g_cast(p, mLj.t[:, :], mL_j.t[:, :], [mL_j], [mLj], tag="mLj")
    g_cast(p, mRj.t[:, :], mR_j.t[:, :], [mR_j], [mRj], tag="mRj")
    s_dma(p, qscb.t[0:64, 0:1], qsc.t.rearrange("(a p) -> p a", p=64),
          [qsc], [qscb], tag="qsc", allow_slow=True)
    s_dma(p, qscb.t[0:64, 1:2], ksc.t.rearrange("(a p) -> p a", p=64),
          [ksc], [qscb], tag="ksc", allow_slow=True)

    def shz_ap(kc, c0, n):
        if kc < 4:
            return zb.t[:, kc, 1 + c0:1 + c0 + n]
        return zb.t[:, kc, c0:c0 + n]

    def shzl_ap(kc, c0, n):
        if kc < 4:
            return zlb.t[:, kc, 1 + c0:1 + c0 + n]
        return zlb.t[:, kc, c0:c0 + n]

    for hp in range(4):
        for j, coff in enumerate((hp * 128, INNER5 + hp * 128, 2 * INNER5 + hp * 128)):
            g_cast(p, wqp.t[:, :, j, :],
                   wqkv.t[:, coff:coff + 128].rearrange("(kc p) o -> p kc o", p=128),
                   [wqkv], [wqp], tag=f"w{hp}_{j}")

        for (j, dstb, has_halo) in ((0, qf, False), (1, kf, True), (2, vf, True)):
            for tt in range(NTT):
                t0 = tt * TT
                pbo = pbq[tt % 2]

                def fmm(eng, j=j, t0=t0, pbo=pbo):
                    last = None
                    for kc in range(DC):
                        lhsT = wqp.t[:, kc, j, :]
                        eng.ldweights(lhsT)
                        last = eng.matmul(pbo.t[0:128, 0:TT], lhsT,
                                          shz_ap(kc, t0, TT),
                                          start=(kc == 0), stop=(kc == DC - 1))
                    return last
                p.step("tensor", fmm, reads=[wqp, zb], writes=[pbo],
                       tag=f"q{hp}mm{j}_{tt}")

                off = 512 if has_halo else 0

                def fev(eng, dstb=dstb, t0=t0, off=off, pbo=pbo):
                    return eng.tensor_copy(dstb.t[:, off + t0:off + t0 + TT],
                                           pbo.t[0:128, 0:TT])
                p.step("vector", fev, reads=[pbo], writes=[dstb],
                       tag=f"q{hp}ev{j}_{tt}")

            if has_halo:
                pbo = pbq[0]

                def fmh(eng, j=j, pbo=pbo):
                    last = None
                    for kc in range(DC):
                        lhsT = wqp.t[:, kc, j, :]
                        eng.ldweights(lhsT)
                        last = eng.matmul(pbo.t[0:128, 0:TT], lhsT,
                                          shzl_ap(kc, 0, 512),
                                          start=(kc == 0), stop=(kc == DC - 1))
                    return last
                p.step("tensor", fmh, reads=[wqp, zlb], writes=[pbo], tag=f"q{hp}mh{j}")

                def fevh(eng, dstb=dstb, pbo=pbo):
                    return eng.tensor_copy(dstb.t[:, 0:512], pbo.t[0:128, 0:TT])
                p.step("vector", fevh, reads=[pbo], writes=[dstb], tag=f"q{hp}evh{j}")

        for g in range(NKC):
            def ftr(eng, g=g):
                return eng.transpose(pbtv.t[0:128, 0:128], vf.t[:, g * 128:(g + 1) * 128],
                                     identb.t[:, :])
            p.step("tensor", ftr, reads=[vf, identb], writes=[pbtv], tag=f"v{hp}tr{g}")

            def fve(eng, g=g):
                dst = vtm.t[:, g, :].rearrange("p (hh d) -> p hh d", hh=2)[:, :, 0:64]
                return eng.tensor_copy(
                    dst, pbtv.t[0:128, 0:128].rearrange("p (hh d) -> p hh d", hh=2))
            p.step("vector", fve, reads=[pbtv], writes=[vtm], tag=f"v{hp}ve{g}")

        def fvo(eng):
            return eng.memset(
                vtm.t[:, :, :].rearrange("p g (hh d) -> p g hh d", hh=2)[:, :, :, 64:65], 1.0)
        p.step("vector", fvo, writes=[vtm], tag=f"v{hp}on")

        for hh in range(2):
            h = hp * 2 + hh
            rb = hh * 64

            for (srcb, isq) in ((qf, True), (kf, False)):
                L = T if isq else TK
                CS = csqb if isq else cskb
                SS = ssqb if isq else sskb
                dst = Qa if isq else Ka
                for ti in range(ceil_div(L, TT)):
                    a0 = ti * TT
                    n = min(TT, L - a0)

                    def fsq(eng, srcb=srcb, rb=rb, a0=a0, n=n):
                        return eng.activation(out=sq64.t[0:64, 0:n],
                                              in_=srcb.t[rb:rb + 64, a0:a0 + n],
                                              func=AF.Square)
                    p.step("scalar", fsq, reads=[srcb], writes=[sq64], tag=f"n{h}q{isq}_{ti}")

                    def fsum(eng, n=n):
                        eng.ldweights(onesc.t[0:64, 0:1])
                        return eng.matmul(pbo1.t[0:1, 0:n], onesc.t[0:64, 0:1],
                                          sq64.t[0:64, 0:n], start=True, stop=True)
                    p.step("tensor", fsum, reads=[sq64, onesc], writes=[pbo1],
                           tag=f"n{h}s{isq}_{ti}")

                    def fsr(eng, n=n):
                        return eng.activation(out=nrm.t[0:1, 0:n],
                                              in_=pbo1.t[0:1, 0:n], func=AF.Sqrt)
                    p.step("scalar", fsr, reads=[pbo1], writes=[nrm], tag=f"n{h}r{isq}_{ti}")

                    def fmx(eng, n=n):
                        return eng.tensor_scalar_max(nrm.t[0:1, 0:n], nrm.t[0:1, 0:n], 1e-12)
                    p.step("vector", fmx, reads=[nrm], writes=[nrm], tag=f"n{h}m{isq}_{ti}")

                    def frc(eng, n=n):
                        return eng.reciprocal(nrm.t[0:1, 0:n], nrm.t[0:1, 0:n])
                    p.step("vector", frc, reads=[nrm], writes=[nrm], tag=f"n{h}c{isq}_{ti}")

                    def fbc(eng, n=n):
                        return eng.matmul(pbo1.t[0:64, 0:n], onesr.t[0:1, 0:64],
                                          nrm.t[0:1, 0:n], start=True, stop=True)
                    p.step("tensor", fbc, reads=[nrm, onesr], writes=[pbo1],
                           tag=f"n{h}b{isq}_{ti}")

                    def fnb(eng, n=n):
                        return eng.tensor_copy(nrmb.t[0:64, 0:n], pbo1.t[0:64, 0:n])
                    p.step("vector", fnb, reads=[pbo1], writes=[nrmb], tag=f"n{h}n{isq}_{ti}")

                    def fq0(eng, srcb=srcb, rb=rb, a0=a0, n=n):
                        return eng.tensor_copy(t64.t[0:64, 0:n],
                                               srcb.t[rb:rb + 64, a0:a0 + n])
                    p.step("vector", fq0, reads=[srcb], writes=[t64],
                           tag=f"n{h}y{isq}_{ti}")

                    def fqn(eng, n=n, isq=isq):
                        sc = qscb.t[0:64, 0:1] if isq else qscb.t[0:64, 1:2]
                        return eng.scalar_tensor_tensor(
                            t64.t[0:64, 0:n], t64.t[0:64, 0:n],
                            sc, nrmb.t[0:64, 0:n], ALU.mult, ALU.mult)
                    p.step("vector", fqn, reads=[t64, qscb, nrmb], writes=[t64],
                           tag=f"n{h}x{isq}_{ti}")

                    def fr1(eng, CS=CS, dst=dst, a0=a0, n=n):
                        return eng.tensor_tensor(dst.t[0:64, a0:a0 + n], t64.t[0:64, 0:n],
                                                 CS.t[0:64, a0:a0 + n], ALU.mult)
                    p.step("vector", fr1, reads=[t64, CS], writes=[dst], tag=f"x{h}a{isq}_{ti}")

                    def fr2(eng, SS=SS, a0=a0, n=n):
                        return eng.tensor_mul(r64.t[0:64, 0:n], t64.t[0:64, 0:n],
                                              SS.t[0:64, a0:a0 + n])
                    p.step("vector", fr2, reads=[t64, SS], writes=[r64], tag=f"x{h}b{isq}_{ti}")

                    def fr3(eng, n=n):
                        eng.tensor_copy(rsw.t[0:32, 0:n], r64.t[32:64, 0:n])
                        return eng.tensor_copy(rsw.t[32:64, 0:n], r64.t[0:32, 0:n])
                    p.step("vector", fr3, reads=[r64], writes=[rsw], tag=f"x{h}c{isq}_{ti}")

                    def fr4(eng, dst=dst, a0=a0, n=n):
                        return eng.tensor_add(dst.t[0:64, a0:a0 + n],
                                              dst.t[0:64, a0:a0 + n], rsw.t[0:64, 0:n])
                    p.step("vector", fr4, reads=[rsw, dst], writes=[dst], tag=f"x{h}d{isq}_{ti}")

            def fqa(eng):
                return eng.memset(Qa.t[64:65, 0:T], 1.0)
            p.step("vector", fqa, writes=[Qa], tag=f"A{h}qa")

            def fka(eng):
                return eng.tensor_copy(Ka.t[64:66, 0:TK], kbb.t[0:2, :])
            p.step("vector", fka, reads=[kbb], writes=[Ka], tag=f"A{h}ka")

            for gs in range(36):
                if gs < 32:
                    ic_m = gs
                    i0 = ic_m * 128

                    def fmx1(eng, i0=i0):
                        lhsT = Qa.t[0:65, i0:i0 + 128]
                        eng.ldweights(lhsT)
                        eng.matmul(pbw.t[0:128, 0:512], lhsT,
                                   Ka.t[0:65, i0:i0 + 512], start=True, stop=True)
                        eng.ldweights(lhsT)
                        return eng.matmul(pbw.t[0:128, 512:640], lhsT,
                                          Ka.t[0:65, i0 + 512:i0 + 640],
                                          start=True, stop=True)
                    p.step("tensor", fmx1, reads=[Qa, Ka], writes=[pbw], tag=f"M{h}m{ic_m}")

                    def fmx2(eng):
                        eng.tensor_add(pbw.t[0:128, 0:128], pbw.t[0:128, 0:128],
                                       mLi.t[:, :])
                        return eng.tensor_add(pbw.t[0:128, 512:640],
                                              pbw.t[0:128, 512:640], mRi.t[:, :])
                    p.step("vector", fmx2, reads=[pbw, mLi, mRi], writes=[pbw],
                           tag=f"M{h}k{ic_m}")

                    def fmx3(eng):
                        return eng.tensor_reduce(negm.t[:, 1:2], pbw.t[0:128, 0:640],
                                                 axis=AX.X, op=ALU.max, negate=True)
                    p.step("vector", fmx3, reads=[pbw], writes=[negm], tag=f"M{h}r{ic_m}")

                    def fmx4(eng):
                        return eng.transpose(pbo1.t[0:2, 256:384], negm.t[:, 0:2],
                                             identf.t[:, :])
                    p.step("tensor", fmx4, reads=[negm, identf], writes=[pbo1],
                           tag=f"M{h}t{ic_m}")

                    def fmx5(eng, i0=i0):
                        return eng.tensor_copy(Qa.t[64:66, i0:i0 + 128],
                                               pbo1.t[0:2, 256:384])
                    p.step("vector", fmx5, reads=[pbo1], writes=[Qa], tag=f"M{h}w{ic_m}")

                g2 = gs - 4
                kb = gs
                ib0 = max(0, g2 * 128)
                ib1 = min(T, g2 * 128 + 640)
                nI = ib1 - ib0
                slot = kb % 6
                poff = ib0 - g2 * 128

                def fsT(eng, kb=kb, ib0=ib0, nI=nI):
                    lhsT = Ka.t[0:66, kb * 128:(kb + 1) * 128]
                    done = 0
                    last = None
                    while done < nI:
                        n = min(512, nI - done)
                        eng.ldweights(lhsT)
                        last = eng.matmul(pbs.t[0:128, done:done + n], lhsT,
                                          Qa.t[0:66, ib0 + done:ib0 + done + n],
                                          start=True, stop=True)
                        done += n
                    return last
                p.step("tensor", fsT, reads=[Ka, Qa], writes=[pbs], tag=f"S{h}m{kb}")

                def fsE(eng, slot=slot, poff=poff, nI=nI):
                    return eng.activation(out=Pt.t[:, slot, poff:poff + nI],
                                          in_=pbs.t[0:128, 0:nI], func=AF.Exp)
                p.step("scalar", fsE, reads=[pbs], writes=[Pt], tag=f"S{h}e{kb}")

                if g2 >= 0:
                    def fsL(eng, slot=slot):
                        return eng.tensor_mul(Pt.t[:, slot, 0:128],
                                              Pt.t[:, slot, 0:128], mLj.t[:, :])
                    p.step("vector", fsL, reads=[Pt, mLj], writes=[Pt], tag=f"S{h}L{kb}")
                if g2 < 28:
                    def fsR(eng, slot=slot):
                        return eng.tensor_mul(Pt.t[:, slot, 512:640],
                                              Pt.t[:, slot, 512:640], mRj.t[:, :])
                    p.step("vector", fsR, reads=[Pt, mRj], writes=[Pt], tag=f"S{h}R{kb}")

                ic = gs - 4
                if 0 <= ic < 32:
                    i0 = ic * 128

                    def fav(eng, hh=hh, ic=ic, i0=i0):
                        last = None
                        for d in range(5):
                            kb2 = ic + d
                            slot2 = kb2 % 6
                            poff2 = i0 - (kb2 - 4) * 128
                            lhsT = vtm.t[:, kb2, hh * 65:hh * 65 + 65]
                            eng.ldweights(lhsT)
                            last = eng.matmul(pbo1.t[0:65, 0:128], lhsT,
                                              Pt.t[:, slot2, poff2:poff2 + 128],
                                              start=(d == 0), stop=(d == 4))
                        return last
                    p.step("tensor", fav, reads=[vtm, Pt], writes=[pbo1], tag=f"V{h}a{ic}")

                    def fde(eng):
                        return eng.tensor_copy(obuf.t[0:65, :], pbo1.t[0:65, 0:128])
                    p.step("vector", fde, reads=[pbo1], writes=[obuf], tag=f"V{h}d{ic}")

                    def fdr(eng):
                        return eng.reciprocal(denr.t[0:1, :], obuf.t[64:65, :])
                    p.step("vector", fdr, reads=[obuf], writes=[denr], tag=f"V{h}r{ic}")

                    def fdb(eng):
                        return eng.matmul(pbo1.t[0:64, 128:256], onesr.t[0:1, 0:64],
                                          denr.t[0:1, :], start=True, stop=True)
                    p.step("tensor", fdb, reads=[denr, onesr], writes=[pbo1],
                           tag=f"V{h}b{ic}")

                    def fon(eng):
                        return eng.tensor_tensor(obuf2.t[0:64, :], obuf.t[0:64, :],
                                                 pbo1.t[0:64, 128:256], ALU.mult)
                    p.step("vector", fon, reads=[obuf, pbo1], writes=[obuf2],
                           tag=f"V{h}o{ic}")

                    def fos(eng, dsem, h=h, i0=i0):
                        eng.dma_start(out=o_sp.t[h * 64:(h + 1) * 64, i0:i0 + 128],
                                      in_=obuf2.t[0:64, :]).then_inc(dsem, 16)
                    p.dma("sync", fos, 1, reads=[obuf2], writes=[o_sp], tag=f"V{h}s{ic}")

    p.handoff([qf, kf, vf], [ztile, oftile])
    for tt in range(NTT):
        t0 = tt * TT

        def fzl(eng, dsem, t0=t0):
            eng.dma_start(out=ztile.t[:, :, :],
                          in_=z_in.t.rearrange("(kc p) t -> p kc t", p=128)[:, :, 1 + t0:1 + t0 + TT]
                          ).then_inc(dsem, 16)
            eng.dma_start(out=oftile.t[:, :, :],
                          in_=o_sp.t.rearrange("(kc p) t -> p kc t", p=128)[:, :, t0:t0 + TT]
                          ).then_inc(dsem, 16)
        p.dma("sync", fzl, 2, reads=[z_in, o_sp], writes=[ztile, oftile], tag=f"E{tt}zl")

        for oc in range(DC):
            pbo = pbq[oc % 2]

            def fmm(eng, oc=oc, pbo=pbo):
                last = None
                for kc in range(4):
                    lhsT = wob.t[:, kc, oc * 128:(oc + 1) * 128]
                    eng.ldweights(lhsT)
                    last = eng.matmul(pbo.t[0:128, 0:TT], lhsT,
                                      oftile.t[:, kc, :],
                                      start=(kc == 0), stop=(kc == 3))
                return last
            p.step("tensor", fmm, reads=[wob, oftile], writes=[pbo], tag=f"E{tt}m{oc}")

            def fad(eng, oc=oc, pbo=pbo):
                return eng.tensor_add(ztile.t[:, oc, :], ztile.t[:, oc, :],
                                      pbo.t[0:128, 0:TT])
            p.step("vector", fad, reads=[pbo, ztile], writes=[ztile], tag=f"E{tt}a{oc}")

        def fzs(eng, dsem, t0=t0):
            eng.dma_start(out=z_out.t.rearrange("(kc p) t -> p kc t", p=128)[:, :, t0:t0 + TT],
                          in_=ztile.t[:, :, :]).then_inc(dsem, 16)
        p.dma("sync", fzs, 1, reads=[ztile], writes=[z_out], tag=f"E{tt}zs")

    p.emit()
    return nc



def build_l1f_ffn():
    nc = bass.Bass("TRN2", target_bir_lowering=False, debug=False, detect_race_conditions=False)
    p = Prog(nc)

    z_in = p.dram("z_in", [DM, 1 + T], F32, kind="ExternalInput")
    w1 = p.dram("w1", [DM, 2 * HID], F32, kind="ExternalInput")
    b1 = p.dram("b1", [2 * HID], F32, kind="ExternalInput")
    w2 = p.dram("w2", [HID, DM], F32, kind="ExternalInput")
    b2 = p.dram("b2", [DM], F32, kind="ExternalInput")
    lg = p.dram("lg", [DM], F32, kind="ExternalInput")
    lb = p.dram("lb", [DM], F32, kind="ExternalInput")
    z_out = p.dram("z_out", [DM, T], F32, kind="ExternalOutput")

    HH = HID // 2
    CH_SIZES = [128] * 10 + [85]

    ln = LNState(p, "ln", inplace_xc=False)
    ones_col = p.sb("ones_col", [128, 1], BF16)
    ones_row = p.sb("ones_row", [1, 128], F32)
    w1a = p.sb("w1a", [128, DC, HH], BF16)
    w1g = p.sb("w1g", [128, DC, HH], BF16)
    w2b = p.sb("w2b", [128, 11, DM], BF16)
    gcols = p.sb("gcols", [128, DC], F32)
    bcols = p.sb("bcols", [128, DC], F32)
    b2c = p.sb("b2c", [128, DC], F32)
    b1a = p.sb("b1a", [128, 11], F32)
    b1g = p.sb("b1g", [128, 11], F32)
    ztile = p.sb("ztile", [128, DC, 1 + TT], F32)
    htile = p.sb("htile", [128, DC, TT], BF16)
    h1 = p.sb("h1", [128, 11, TT], BF16)
    gt = p.sb("gt", [128, TT], BF16)

    pb = [p.psum(f"pb{i}", [128, 512], F32) for i in range(6)]

    def f1(eng):
        return eng.memset(ones_col.t[:, :], 1.0)
    p.step("vector", f1, writes=[ones_col], tag="oc")

    def f2(eng):
        return eng.memset(ones_row.t[:, :], 1.0)
    p.step("vector", f2, writes=[ones_row], tag="or")

    load_bias_cols(p, gcols, lg, DM, "lg")
    load_bias_cols(p, bcols, lb, DM, "lb")
    load_bias_cols(p, b2c, b2, DM, "b2")

    for half in range(2):
        h0 = half * HH
        g_cast(p, w1a.t[:, :, :],
               w1.t[:, h0:h0 + HH].rearrange("(kc p) o -> p kc o", p=128),
               [w1], [w1a], tag=f"w1a{half}")
        g_cast(p, w1g.t[:, :, :],
               w1.t[:, HID + h0:HID + h0 + HH].rearrange("(kc p) o -> p kc o", p=128),
               [w1], [w1g], tag=f"w1g{half}")
        g_cast(p, w2b.t[:, 0:10, :],
               w2.t[h0:h0 + 1280, :].rearrange("(kc p) o -> p kc o", p=128),
               [w2], [w2b], tag=f"w2a{half}")
        g_cast(p, w2b.t[0:85, 10, :], w2.t[h0 + 1280:h0 + HH, :],
               [w2], [w2b], tag=f"w2c{half}")

        def fb1a(eng, dsem, h0=h0):
            eng.dma_start(out=b1a.t[0:128, 0:10],
                          in_=b1.t[h0:h0 + 1280].rearrange("(kc p) -> p kc", p=128),
                          allow_slow_non_contiguous=True).then_inc(dsem, 16)
            eng.dma_start(out=b1a.t[0:85, 10:11],
                          in_=b1.t[h0 + 1280:h0 + HH].rearrange("(a p) -> p a", p=85),
                          allow_slow_non_contiguous=True).then_inc(dsem, 16)
            eng.dma_start(out=b1g.t[0:128, 0:10],
                          in_=b1.t[HID + h0:HID + h0 + 1280].rearrange("(kc p) -> p kc", p=128),
                          allow_slow_non_contiguous=True).then_inc(dsem, 16)
            eng.dma_start(out=b1g.t[0:85, 10:11],
                          in_=b1.t[HID + h0 + 1280:HID + h0 + HH].rearrange("(a p) -> p a", p=85),
                          allow_slow_non_contiguous=True).then_inc(dsem, 16)
        p.dma("sync", fb1a, 4, reads=[b1], writes=[b1a, b1g], tag=f"b1{half}")

        for tt in range(NTT):
            t0 = tt * TT

            def fzl(eng, dsem, t0=t0):
                eng.dma_start(out=ztile.t[:, :, :],
                              in_=z_in.t.rearrange("(kc p) t -> p kc t", p=128)[:, :, t0:t0 + 1 + TT]
                              ).then_inc(dsem, 16)
            p.dma("sync", fzl, 1, reads=[z_in], writes=[ztile], tag=f"F{half}z{tt}")

            def xs_ap(kc):
                if kc < 4:
                    return ztile.t[:, kc, 1:1 + TT]
                return ztile.t[:, kc, 0:TT]

            def z_ap(kc):
                return htile.t[:, kc, :]

            emit_ln_tile(p, ln, xs_ap, ztile, z_ap, htile, 0, pb[0], pb[1],
                         ones_col, ones_row, g_cols=gcols, b_cols=bcols,
                         tag=f"F{half}l{tt}_")

            if half == 1:
                def fzp(eng, dsem, t0=t0):
                    eng.dma_start(out=ztile.t[:, :, 1:1 + TT],
                                  in_=z_out.t.rearrange("(kc p) t -> p kc t", p=128)[:, :, t0:t0 + TT]
                                  ).then_inc(dsem, 16)
                p.dma("sync", fzp, 1, reads=[z_out], writes=[ztile], tag=f"F1p{tt}")

            co = 0
            for ci, csz in enumerate(CH_SIZES):
                pba = pb[2 + (ci % 2) * 2]
                pbg = pb[3 + (ci % 2) * 2]

                def fma(eng, co=co, csz=csz, pba=pba):
                    last = None
                    for kc in range(DC):
                        lhsT = w1a.t[:, kc, co:co + csz]
                        eng.ldweights(lhsT)
                        last = eng.matmul(pba.t[0:csz, 0:TT], lhsT, htile.t[:, kc, :],
                                          start=(kc == 0), stop=(kc == DC - 1))
                    return last
                p.step("tensor", fma, reads=[w1a, htile], writes=[pba],
                       tag=f"F{half}a{tt}_{ci}")

                def fmg(eng, co=co, csz=csz, pbg=pbg):
                    last = None
                    for kc in range(DC):
                        lhsT = w1g.t[:, kc, co:co + csz]
                        eng.ldweights(lhsT)
                        last = eng.matmul(pbg.t[0:csz, 0:TT], lhsT, htile.t[:, kc, :],
                                          start=(kc == 0), stop=(kc == DC - 1))
                    return last
                p.step("tensor", fmg, reads=[w1g, htile], writes=[pbg],
                       tag=f"F{half}g{tt}_{ci}")

                def fge(eng, ci=ci, csz=csz, pbg=pbg):
                    return eng.activation(out=gt.t[0:csz, :], in_=pbg.t[0:csz, 0:TT],
                                          func=AF.Gelu, bias=b1g.t[0:csz, ci:ci + 1])
                p.step("scalar", fge, reads=[pbg, b1g], writes=[gt],
                       tag=f"F{half}e{tt}_{ci}")

                def fh1(eng, ci=ci, csz=csz, pba=pba):
                    return eng.scalar_tensor_tensor(
                        h1.t[0:csz, ci, :], pba.t[0:csz, 0:TT],
                        b1a.t[0:csz, ci:ci + 1], gt.t[0:csz, :], ALU.add, ALU.mult)
                p.step("vector", fh1, reads=[pba, b1a, gt], writes=[h1],
                       tag=f"F{half}h{tt}_{ci}")
                co += csz

            for oc in range(DC):
                pbo = pb[2 + (oc % 2)]

                def fm2(eng, oc=oc, pbo=pbo):
                    last = None
                    for ci, csz in enumerate(CH_SIZES):
                        lhsT = w2b.t[0:csz, ci, oc * 128:(oc + 1) * 128]
                        eng.ldweights(lhsT)
                        last = eng.matmul(pbo.t[0:128, 0:TT], lhsT, h1.t[0:csz, ci, :],
                                          start=(ci == 0), stop=(ci == len(CH_SIZES) - 1))
                    return last
                p.step("tensor", fm2, reads=[w2b, h1], writes=[pbo],
                       tag=f"F{half}w{tt}_{oc}")

                if half == 0:
                    def fad(eng, oc=oc, pbo=pbo):
                        return eng.tensor_add(ztile.t[:, oc, 1:1 + TT],
                                              ztile.t[:, oc, 1:1 + TT],
                                              pbo.t[0:128, 0:TT])
                    p.step("vector", fad, reads=[pbo, ztile], writes=[ztile],
                           tag=f"F{half}d{tt}_{oc}")
                else:
                    def fad(eng, oc=oc, pbo=pbo):
                        eng.tensor_add(ztile.t[:, oc, 1:1 + TT],
                                       ztile.t[:, oc, 1:1 + TT], pbo.t[0:128, 0:TT])
                        return eng.tensor_scalar_add(ztile.t[:, oc, 1:1 + TT],
                                                     ztile.t[:, oc, 1:1 + TT],
                                                     b2c.t[:, oc:oc + 1])
                    p.step("vector", fad, reads=[pbo, ztile, b2c], writes=[ztile],
                           tag=f"F{half}d{tt}_{oc}")

            def fzs(eng, dsem, t0=t0):
                eng.dma_start(out=z_out.t.rearrange("(kc p) t -> p kc t", p=128)[:, :, t0:t0 + TT],
                              in_=ztile.t[:, :, 1:1 + TT]).then_inc(dsem, 16)
            p.dma("sync", fzs, 1, reads=[ztile], writes=[z_out], tag=f"F{half}s{tt}")

    p.emit()
    return nc



_L1F_NCS = {}


def get_l1f(name):
    if name not in _L1F_NCS:
        _L1F_NCS[name] = {"begin": build_l1f_begin, "attn": build_l1f_attn,
                          "ffn": build_l1f_ffn, "end": build_l1f_end}[name]()
    return _L1F_NCS[name]


def l1f_consts(half):
    pos0 = half * T
    d = np.arange(64)
    fi = d % 32
    inv_freq = 1.0 / 10000.0 ** (np.arange(0, DHEAD, 2, dtype=np.float64) / DHEAD)
    base = (np.arange(0, DHEAD, 2, dtype=np.float64) + 0.4 * DHEAD) / (1.4 * DHEAD)

    def tables(pos, is_q):
        t = pos.astype(np.float64)
        fr = t[None, :] * inv_freq[fi][:, None]
        power = (t - 8192 // 2) / float(WIN // 2)
        scale = base[fi][:, None] ** power[None, :]
        if is_q:
            cs = np.cos(fr) * scale * 8.0
            sn = np.sin(fr) * scale * 8.0
        else:
            cs = np.cos(fr) / scale
            sn = np.sin(fr) / scale
        ss = np.empty_like(sn)
        ss[:32] = sn[:32]
        ss[32:] = -sn[32:]
        return cs.astype(np.float32), ss.astype(np.float32)

    qpos = np.arange(pos0, pos0 + T)
    kpos = np.arange(pos0 - 512, pos0 + T)
    csq_, ssq_ = tables(qpos, True)
    csk_, ssk_ = tables(kpos, False)
    ii = np.arange(128)[:, None]
    jj = np.arange(128)[None, :]
    c = {
        "c_csq": csq_, "c_ssq": ssq_, "c_csk": csk_, "c_ssk": ssk_,
        "c_mLi": np.where(jj >= ii, 0.0, -1e30).astype(np.float32),
        "c_mRi": np.where(jj <= ii, 0.0, -1e30).astype(np.float32),
        "c_mLj": np.where(jj.T <= ii.T, 1.0, 0.0).astype(np.float32).T,
        "c_mRj": np.where(jj.T >= ii.T, 1.0, 0.0).astype(np.float32).T,
        "c_ident": np.eye(128, dtype=np.float32),
    }
    mj = np.arange(128)[:, None]
    mi = np.arange(128)[None, :]
    c["c_mLj"] = (mj <= mi).astype(np.float32)
    c["c_mRj"] = (mj >= mi).astype(np.float32)
    kb = np.zeros((2, 512 + T), np.float32)
    if half == 0:
        kb[0, 0:512] = -1e30
    kb[1, :] = 1.0
    c["kbias"] = kb
    return c


def run_l1ffn_call(x_shards, P):
    nc_b = get_l1f("begin")
    maps = [{"x_in": x_shards[c]} for c in range(NCORES)]
    res = run_bass_kernel_spmd(nc_b, maps, core_ids=list(range(NCORES)))
    z = [np.asarray(res.results[c]["z_out"]) for c in range(NCORES)]

    cst = [l1f_consts(c % 2) for c in range(NCORES)]
    zero_col = np.zeros((DM, 1), np.float32)
    zero_left = np.zeros((DM, 513), np.float32)

    for l in range(2):
        nc_a = get_l1f("attn")
        maps = []
        for c in range(NCORES):
            if c % 2 == 1:
                bcol = z[c - 1][:, T - 1:T]
                zl = z[c - 1][:, T - 513:T]
            else:
                bcol = zero_col
                zl = zero_left
            maps.append({
                "z_in": np.ascontiguousarray(np.concatenate([bcol, z[c]], axis=1)),
                "zleft": np.ascontiguousarray(zl),
                "wqkv": P["L1_ffn_wqkv"][l], "qsc": P["L1_ffn_q_scale"][l],
                "ksc": P["L1_ffn_k_scale"][l], "wo": P["L1_ffn_wo"][l],
                **cst[c],
            })
        res = run_bass_kernel_spmd(nc_a, maps, core_ids=list(range(NCORES)))
        z = [np.asarray(res.results[c]["z_out"]) for c in range(NCORES)]

        nc_f = get_l1f("ffn")
        maps = []
        for c in range(NCORES):
            bcol = z[c - 1][:, T - 1:T] if c % 2 == 1 else zero_col
            maps.append({
                "z_in": np.ascontiguousarray(np.concatenate([bcol, z[c]], axis=1)),
                "w1": P["L1_ffn_w1"][l], "b1": P["L1_ffn_b1"][l],
                "w2": P["L1_ffn_w2"][l], "b2": P["L1_ffn_b2"][l],
                "lg": P["L1_ffn_ln_g"][l], "lb": P["L1_ffn_ln_b"][l],
            })
        res = run_bass_kernel_spmd(nc_f, maps, core_ids=list(range(NCORES)))
        z = [np.asarray(res.results[c]["z_out"]) for c in range(NCORES)]

    nc_e = get_l1f("end")
    maps = [{"x_in": x_shards[c], "z_in": z[c]} for c in range(NCORES)]
    res = run_bass_kernel_spmd(nc_e, maps, core_ids=list(range(NCORES)))
    return [np.asarray(res.results[c]["x_out"]) for c in range(NCORES)]




def make_consts():
    return {"c_ident": np.eye(128, dtype=np.float32)}


def shard_x_fm(x):
    out = []
    for b in range(4):
        for h in range(2):
            out.append(np.ascontiguousarray(x[b, h * T:(h + 1) * T].T))
    return out


def unshard_x_fm(shards):
    x = np.empty((4, 2 * T, DM), np.float32)
    for b in range(4):
        for h in range(2):
            x[b, h * T:(h + 1) * T] = np.asarray(shards[2 * b + h]).T
    return x


_FFN0_NC = None


def get_ffn0():
    global _FFN0_NC
    if _FFN0_NC is None:
        _FFN0_NC = build_ffn0()
    return _FFN0_NC


def run_ffn0(x_shards, P):
    nc = get_ffn0()
    consts = make_consts()
    in_maps = []
    for c in range(NCORES):
        in_maps.append({
            "x_in": x_shards[c],
            "Wf": P["L0_ffn_W"], "bf": P["L0_ffn_b"],
            "V_rf": P["L0_ffn_V_r"], "Cf": P["L0_ffn_C"],
            **consts,
        })
    res = run_bass_kernel_spmd(nc, in_maps, core_ids=list(range(NCORES)))
    return [np.asarray(r["x_out"]) for r in res.results]



def kernel(**inputs):
    P = inputs
    x = np.asarray(P["L0_attn_x"], np.float32)
    shards = shard_x_fm(x)

    shards = run_ffn0(shards, P)
    shards = run_l0attn(shards, P)
    shards = run_ffn0(shards, P)
    shards = run_l1ffn_call(shards, P)
    shards = run_l1attn(shards, P)
    shards = run_l1ffn_call(shards, P)

    return unshard_x_fm(shards)
